# version 02 (final)
# baseline (speedup 1.0000x reference)
import os
import numpy as np
import ml_dtypes
import concourse.bass as bass
import concourse.mybir as mybir
from concourse.bass_utils import run_bass_kernel_spmd
from concourse.bass_isa_rust import ReduceOp

F32 = mybir.dt.float32
BF16 = mybir.dt.bfloat16
I32 = mybir.dt.int32
OP = mybir.AluOpType
ACT = mybir.ActivationFunctionType
AX = mybir.AxisListType.X

NLOC = 2048
SFULL = 4096
DM = 1024
NDT = 8

FULL_SEQUENCE = [
    (0, "attn_a"), (0, "ffn"), (0, "attn_b"), (0, "ffn"),
    (1, "attn_a"), (1, "ffn"), (1, "attn_b"), (1, "ffn"),
    (2, "attn_a"), (2, "ffn"), (2, "attn_b"), (2, "ffn"),
    (3, "attn_a"), (3, "ffn"), (3, "attn_b"), (3, "ffn"),
]
ACTIVE = list(FULL_SEQUENCE)
SIM_NO_CC = False
DEBUG_TENSORS = {}
DEBUG_DUMP = []



def host_prep(inputs, b, hc):
    T = lambda a: np.ascontiguousarray(np.asarray(a).T)
    C = np.ascontiguousarray
    im = {}
    x = np.asarray(inputs["L0_attn_a_x"])
    im["x"] = T(x[b, hc * NLOC:(hc + 1) * NLOC, :])

    for nm in ["wq", "wk", "wv", "wo"]:
        im[f"l0a_{nm}w"] = T(inputs[f"L0_attn_a_{nm}_w"])
        im[f"l0a_{nm}b"] = C(np.asarray(inputs[f"L0_attn_a_{nm}_b"]).reshape(NDT, 128, 1))
    im["l0a_wvb_row"] = C(np.asarray(inputs["L0_attn_a_wv_b"]).reshape(8, 128))
    im["l0f_router"] = T(inputs["L0_ffn_w_router"])
    im["l0f_fc"] = C(np.transpose(np.asarray(inputs["L0_ffn_w_fc"]), (0, 2, 1)))
    im["l0f_proj"] = C(np.transpose(np.asarray(inputs["L0_ffn_w_proj"]), (0, 2, 1)))
    im["l0b_wqkv"] = T(inputs["L0_attn_b_W_qkv"])

    im["l1a_wqkv"] = T(inputs["L1_attn_a_Wqkv_w"])
    im["l1a_bqkv"] = C(np.asarray(inputs["L1_attn_a_Wqkv_b"]).reshape(24, 128, 1))
    im["l1a_gnw"] = C(np.asarray(inputs["L1_attn_a_gn_w"]).reshape(NDT, 128, 1))
    im["l1a_gnb"] = C(np.asarray(inputs["L1_attn_a_gn_b"]).reshape(NDT, 128, 1))
    im["l1a_outw"] = T(inputs["L1_attn_a_out_w"])
    im["l1a_outb"] = C(np.asarray(inputs["L1_attn_a_out_b"]).reshape(NDT, 128, 1))
    im["l1a_kb_row"] = C(np.asarray(inputs["L1_attn_a_Wqkv_b"])[1024:2048].reshape(1, 1024))
    im["l1a_vb_row"] = C(np.asarray(inputs["L1_attn_a_Wqkv_b"])[2048:3072].reshape(1, 1024))
    ex32 = np.zeros((32, 8 * 128), np.float32)
    for d in range(8):
        for m in range(128):
            ex32[d * 4 + m // 32, d * 128 + m] = 1.0
    im["l1a_exp"] = ex32
    im["repmat32"] = C(np.tile(np.eye(32, dtype=np.float32), (1, 4)))

    im["l1f_w1"] = C(inputs["L1_ffn_W1"])
    im["l1f_b1"] = C(np.asarray(inputs["L1_ffn_b1"]).reshape(32, 128, 1))
    im["l1f_a1"] = C(np.asarray(inputs["L1_ffn_A1"]).transpose(1, 0, 2).reshape(1024, 24))
    im["l1f_b1m"] = C(np.asarray(inputs["L1_ffn_B1"]).reshape(24, 4096))
    im["l1f_r1"] = C(inputs["L1_ffn_R1"])
    im["l1f_w2"] = C(inputs["L1_ffn_W2"])
    im["l1f_b2"] = C(np.asarray(inputs["L1_ffn_b2"]).reshape(NDT, 128, 1))
    im["l1f_a2"] = C(np.asarray(inputs["L1_ffn_A2"]).transpose(1, 0, 2).reshape(4096, 24))
    im["l1f_b2m"] = C(np.asarray(inputs["L1_ffn_B2"]).reshape(24, 1024))
    im["l1f_r2"] = C(inputs["L1_ffn_R2"])

    for d in ["f", "b"]:
        im[f"mb_win_{d}"] = T(inputs[f"L1_attn_b_win_{d}"])
        im[f"mb_wconv_{d}"] = C(np.asarray(inputs[f"L1_attn_b_wconv_{d}"])
                                .reshape(16, 128, 4).transpose(1, 0, 2).reshape(128, 64))
        im[f"mb_bconv_{d}"] = C(np.asarray(inputs[f"L1_attn_b_bconv_{d}"]).reshape(16, 128).T)
        im[f"mb_wx_{d}"] = T(inputs[f"L1_attn_b_wx_{d}"])
        im[f"mb_wdt_{d}"] = T(inputs[f"L1_attn_b_wdt_{d}"])
        im[f"mb_bdt_{d}"] = C(np.asarray(inputs[f"L1_attn_b_bdt_{d}"]).reshape(16, 128).T)
        im[f"mb_alog_{d}"] = C(np.asarray(inputs[f"L1_attn_b_Alog_{d}"])
                               .reshape(16, 128, 16).transpose(1, 0, 2).reshape(128, 256))
        im[f"mb_dp_{d}"] = C(np.asarray(inputs[f"L1_attn_b_Dp_{d}"]).reshape(16, 128).T)
        im[f"mb_wout_{d}"] = T(inputs[f"L1_attn_b_wout_{d}"])
    im["mb_wproj"] = T(inputs["L1_attn_b_wproj"])
    rs32 = np.zeros((32, 32 * 128), np.float32)
    for n in range(32):
        rs32[n, n * 128:(n + 1) * 128] = 1.0
    im["rowsel32"] = rs32
    im["mb_bproj"] = C(np.asarray(inputs["L1_attn_b_bproj"]).reshape(NDT, 128).T)

    im["l2a_wqkv"] = C(inputs["L2_attn_a_w_qkv"])
    im["l2a_wo"] = C(inputs["L2_attn_a_wo"])
    for nm in ["lambda_q1", "lambda_q2", "lambda_k1", "lambda_k2"]:
        im[f"l2a_{nm}"] = C(np.asarray(inputs[f"L2_attn_a_{nm}"]).reshape(64, 1))
    im["l2a_normw"] = C(np.asarray(inputs["L2_attn_a_norm_w"]).reshape(16, 64, 1))
    sel = np.zeros((64, 32), np.float32)
    sel[32 * hc + np.arange(32), np.arange(32)] = 1.0
    im["l2a_sel"] = sel

    im["l2f_gate"] = T(inputs["L2_ffn_w_gate"])
    im["l2f_up"] = T(inputs["L2_ffn_w_up"])
    im["l2f_down"] = T(inputs["L2_ffn_w_down"])

    for nm in ["q", "k", "v"]:
        im[f"l2b_w{nm}"] = C(inputs[f"L2_attn_b_W{nm}"])
        im[f"l2b_b{nm}"] = C(np.asarray(inputs[f"L2_attn_b_b{nm}"]).reshape(NDT, 128, 1))
    rel = np.asarray(inputs["L2_attn_b_rel_emb"])
    ext = rel[:, np.minimum(np.arange(8191), 4094)]
    pp, ff = np.mgrid[0:128, 0:128]
    dgn = np.arange(47) - 31 + 16 * hc
    idx = np.clip(4095 + 128 * dgn[:, None, None] + ff[None] - pp[None], 0, 8190)
    dtb = ext[:, idx]
    im["l2b_diag"] = C(dtb.transpose(0, 2, 1, 3).reshape(16, 128, 47 * 128))

    for nm in ["q", "k", "v", "o"]:
        im[f"l3a_w{nm}"] = T(inputs[f"L3_attn_a_W{nm}"])
        im[f"l3a_b{nm}"] = C(np.asarray(inputs[f"L3_attn_a_b{nm}"]).reshape(NDT, 128, 1))
    im["l3a_lambda"] = C(np.asarray(inputs["L3_attn_a_prior_lambda"]).reshape(1, 1))
    dist = np.full((16, 6, 128, 128), 1.0e6, dtype=np.float32)
    for qi in range(16):
        qg0 = hc * 2048 + 128 * qi
        Jlist = [qi - 1, qi, qi + 1, qi + 15, qi + 16, qi + 17]
        for slot, j in enumerate(Jlist):
            if 0 <= j < 32 and abs(j - (qi + 16 * hc)) <= 1:
                dist[qi, slot] = np.abs((qg0 + pp) - (128 * j + ff)).astype(np.float32)
    im["l3a_dist"] = C(dist)
    im["onesrowf"] = np.ones((1, 128), np.float32)

    im["l3f_gate"] = C(inputs["L3_ffn_gate_W"])
    im["l3f_w1"] = C(inputs["L3_ffn_W1"])
    im["l3f_b1"] = C(np.asarray(inputs["L3_ffn_b1"]).reshape(8, 16, 128, 1))
    im["l3f_w2"] = C(inputs["L3_ffn_W2"])
    im["l3f_b2r"] = C(np.asarray(inputs["L3_ffn_b2"]))
    for nm in ["q", "k", "v", "o"]:
        im[f"l3b_w{nm}"] = C(inputs[f"L3_attn_b_W{nm}"])
        im[f"l3b_b{nm}"] = C(np.asarray(inputs[f"L3_attn_b_b{nm}"]).reshape(NDT, 128, 1))

    rs8 = np.zeros((8, 8 * 128), np.float32)
    for e in range(8):
        rs8[e, e * 128:(e + 1) * 128] = 1.0
    im["rowsel8"] = rs8
    p24 = np.zeros((3, 24), np.float32)
    for e in range(3):
        p24[e, e * 8:(e + 1) * 8] = 1.0
    im["pat24"] = p24
    cs64 = np.zeros((64, 64 * 64), np.float32)
    for cm in range(64):
        cs64[cm, cm * 64:(cm + 1) * 64] = 1.0
    im["colsel64"] = cs64
    im["ones128"] = np.ones((128, 128), np.float32)
    im["eye128"] = np.eye(128, dtype=np.float32)
    im["eye128f"] = np.eye(128, dtype=np.float32)
    im["onescolf"] = np.ones((128, 1), np.float32)
    im["smask"] = np.triu(np.ones((128, 128), np.float32))
    im["mask_rem"] = np.full((128, 1), float(hc), np.float32)
    im["mask_rem_b"] = np.full((128, 1), float(1 - hc), np.float32)
    if os.environ.get("KMASK0") == "1":
        im["mask_rem_b"] = np.zeros((128, 1), np.float32)
    return im



ENG2SEM = {"tensor": "T", "vector": "V", "scalar": "S", "gpsimd": "G", "sync": "Y"}


class Prog:
    SEMS = ["W", "X", "T", "V", "S", "G", "C", "O", "Y"]

    def __init__(self, nc, block):
        self.nc = nc
        self.block = block
        self.sems = {s: nc.alloc_semaphore(f"sem{s}") for s in self.SEMS}
        self.cnt = {s: 0 for s in self.SEMS}

    def on(self, eng, fn):
        getattr(self.block, eng)(fn)

    def inc(self, name, n=1):
        self.cnt[name] += n
        return self.cnt[name]

    def val(self, name):
        return self.cnt[name]

    def snap(self, *names):
        if not names:
            names = ("T", "V", "S", "G", "X", "W", "C", "Y")
        return {n: self.cnt[n] for n in names}

    def stage(self, eng, emit_fn, waits=None):
        sem = ENG2SEM[eng]
        w = dict(waits or {})

        def f(e):
            for nm, v in w.items():
                if v > 0:
                    e.wait_ge(self.sems[nm], v)
            last = emit_fn(e)
            if isinstance(last, (list, tuple)):
                last = last[-1]
            assert last is not None, "stage emit_fn must return its last instruction"
            last.then_inc(self.sems[sem], 1)
        self.on(eng, f)
        return self.inc(sem)

    def dma(self, emit_fn, waits=None, n=1, eng="gpsimd", sem="W"):
        w = dict(waits or {})
        semh = self.sems[sem]

        def f(e):
            for nm, v in w.items():
                if v > 0:
                    e.wait_ge(self.sems[nm], v)
            emit_fn(e, lambda ins: ins.then_inc(semh, 16))
        self.on(eng, f)
        return self.inc(sem, 16 * n)


class Arena:
    def __init__(self, nc, nbytes):
        self.t = nc.alloc_sbuf_tensor("arena", [128, nbytes], mybir.dt.uint8)
        self.top = 0
        self.nbytes = nbytes
        self.marks = []

    def alloc(self, nelem, dtype):
        esz = {F32: 4, BF16: 2, I32: 4}[dtype]
        off = (self.top + esz - 1) // esz * esz
        self.top = off + nelem * esz
        assert self.top <= self.nbytes, f"arena overflow: {self.top} > {self.nbytes}"
        return self.t[:, off:off + nelem * esz].bitcast(dtype)

    def push(self):
        self.marks.append(self.top)

    def pop(self):
        self.top = self.marks.pop()


def dt_sl(buf, i, width, c0=0, c1=None):
    c1 = width if c1 is None else c1
    return buf[:, i * width + c0: i * width + c1]


def mm_acc(t, psum_ap, lhsT_aps, rhs_aps, start=True, stop=True):
    n = len(lhsT_aps)
    last = None
    for k in range(n):
        last = t.matmul(psum_ap, lhsT_aps[k], rhs_aps[k],
                        start=(start and k == 0), stop=(stop and k == n - 1))
    return last


def load_wT(g, mark, dst, src, ktiles, width, dst_off=0):
    src_r = src.rearrange("(n p) m -> p n m", p=128)
    dst_r = dst[:, dst_off:dst_off + ktiles * width].rearrange("p (n m) -> p n m", n=ktiles)
    mark(g.dma_start(out=dst_r, in_=src_r))



def build_program(im_sample):
    nc = bass.Bass("TRN2", target_bir_lowering=False, detect_race_conditions=False)
    A = {}
    for k, v in im_sample.items():
        dt = {np.dtype(np.float32): F32, np.dtype(np.int32): I32}[v.dtype]
        A[k] = nc.dram_tensor(k, list(v.shape), dt, kind="ExternalInput").ap()
    yT = nc.dram_tensor("yT", [DM, NLOC], F32, kind="ExternalOutput").ap()
    dbg = nc.dram_tensor("dbg", [8, 128, 2048], F32, kind="ExternalOutput").ap()

    cc = {
        "in_big": nc.dram_tensor("cc_in_big", [DM, NLOC], BF16, kind="Internal"),
        "out_big": nc.dram_tensor("cc_out_big", [2 * DM, NLOC], BF16, kind="Internal"),
        "in_small": nc.dram_tensor("cc_in_small", [128, 2048], F32, kind="Internal"),
        "out_small": nc.dram_tensor("cc_out_small", [256, 2048], F32, kind="Internal"),
        "groups": [[0, 1], [2, 3], [4, 5], [6, 7]],
    }

    xT = nc.alloc_sbuf_tensor("xT", [128, NDT * NLOC], F32)
    xn = nc.alloc_sbuf_tensor("xn", [128, NDT * NLOC], BF16)
    consts = nc.alloc_sbuf_tensor("consts", [128, 1440], BF16)
    constf = nc.alloc_sbuf_tensor("constf", [128, 320], F32)
    lnstat = nc.alloc_sbuf_tensor("lnstat", [128, 2048], F32)
    arena = Arena(nc, 99 * 1024)
    PS = [nc.alloc_psum_tensor(f"ps{i}", [128, 512], F32) for i in range(8)]

    sbrec = dict(xT=xT, xn=xn, consts=consts, constf=constf, arena=arena, lnstat=lnstat)

    with nc.Block() as block:
        p = Prog(nc, block)
        st = dict(A=A, sb=sbrec, PS=PS, PS2=[PS[6], PS[7]], cc=cc, yT=yT, dbg=dbg)
        emit_prologue(p, st)
        emitters = {
            "attn_a": [emit_l0_attn_a, emit_l1_attn_a, emit_l2_attn_a, emit_l3_attn_a],
            "attn_b": [emit_l0_attn_b, emit_l1_attn_b, emit_l2_attn_b, emit_l3_attn_b],
            "ffn": [emit_l0_ffn, emit_l1_ffn, emit_l2_ffn, emit_l3_ffn],
        }
        for (layer, kind) in ACTIVE:
            if kind == "nop":
                continue
            emit_ln(p, st)
            if kind == "ln":
                continue
            emitters[kind][layer](p, st)
        emit_epilogue(p, st)
    return nc


def emit_prologue(p, st):
    A, sb = st["A"], st["sb"]

    def f(g, mark):
        mark(g.dma_start(out=sb["consts"][:, 0:128], in_=A["ones128"]))
        mark(g.dma_start(out=sb["consts"][:, 128:256], in_=A["eye128"]))
        mark(g.dma_start(out=sb["consts"][:, 256:384], in_=A["smask"]))
        mark(g.dma_start(out=sb["consts"][0:8, 384:1408], in_=A["rowsel8"]))
        mark(g.dma_start(out=sb["consts"][0:3, 1408:1432], in_=A["pat24"]))
        mark(g.dma_start(out=sb["constf"][:, 0:1], in_=A["mask_rem"]))
        mark(g.dma_start(out=sb["constf"][:, 1:2], in_=A["mask_rem_b"]))
        mark(g.dma_start(out=sb["constf"][:, 2:3], in_=A["onescolf"]))
        mark(g.dma_start(out=sb["constf"][:, 32:160], in_=A["eye128f"]))
        mark(g.dma_start(out=sb["constf"][0:1, 160:288], in_=A["onesrowf"]))
        for i in range(NDT):
            mark(g.dma_start(out=dt_sl(sb["xT"], i, NLOC), in_=A["x"][i * 128:(i + 1) * 128, :]))
    x0 = p.dma(f, n=10 + NDT, sem="X")
    p.stage("vector", lambda v: [
        v.memset(sb["constf"][:, 5:6], -1.0),
        v.memset(sb["constf"][:, 6:7], -30.0),
        v.memset(sb["constf"][:, 7:8], 1.0e-6)][-1], waits={"X": x0})


def emit_epilogue(p, st):
    sb, yT = st["sb"], st["yT"]
    snap = p.snap()

    def f(sy):
        for nm, v in snap.items():
            if v > 0:
                sy.wait_ge(p.sems[nm], v)
        for i in range(NDT):
            sy.dma_start(out=yT[i * 128:(i + 1) * 128, :],
                         in_=dt_sl(sb["xT"], i, NLOC)).then_inc(p.sems["O"], 16)
        sy.wait_ge(p.sems["O"], 16 * NDT)
    p.on("sync", f)
    p.inc("O", 16 * NDT)
    ndump = len(DEBUG_DUMP)

    def fd(g):
        for nm, v in snap.items():
            if v > 0:
                g.wait_ge(p.sems[nm], v)
        used = {}
        for slot, ap in DEBUG_DUMP:
            pn, fn = ap.shape[0], ap.shape[1]
            c0 = used.get(slot, 0)
            used[slot] = c0 + fn
            g.dma_start(out=st["dbg"][slot][0:pn, c0:c0 + fn], in_=ap).then_inc(p.sems["O"], 16)
        if ndump:
            g.wait_ge(p.sems["O"], 16 * NDT + 16 * ndump)
    p.on("gpsimd", fd)
    p.inc("O", 16 * ndump)
    DEBUG_DUMP.clear()



def emit_ln(p, st):
    sb, PS = st["sb"], st["PS"]
    xT, xn, consts, ar = sb["xT"], sb["xn"], sb["consts"], sb["arena"]
    ones = consts[:, 0:128]
    pre = p.snap()

    ar.push()
    sq = ar.alloc(NDT * NLOC, BF16)
    mB = ar.alloc(NLOC, F32)
    rB = ar.alloc(NLOC, F32)
    tmp = ar.alloc(NLOC, F32)

    v_copy = p.stage("vector", lambda v: [
        v.tensor_copy(dt_sl(xn, i, NLOC), dt_sl(xT, i, NLOC)) for i in range(NDT)
    ], waits=pre)

    s_sq = p.stage("scalar", lambda s: [
        s.activation(dt_sl(sq, i, NLOC), dt_sl(xT, i, NLOC), ACT.Square) for i in range(NDT)
    ], waits=pre)

    def ft(t):
        last = None
        for c in range(4):
            mm_acc(t, PS[c][:, :], [ones] * NDT,
                   [dt_sl(xn, i, NLOC, c * 512, (c + 1) * 512) for i in range(NDT)])
            last = mm_acc(t, PS[4 + c][:, :], [ones] * NDT,
                          [dt_sl(sq, i, NLOC, c * 512, (c + 1) * 512) for i in range(NDT)])
        return last
    t_sums = p.stage("tensor", ft, waits={"V": v_copy, "S": s_sq})

    def fv2(v):
        last = None
        for c in range(4):
            cs = slice(c * 512, (c + 1) * 512)
            v.tensor_scalar_mul(mB[:, cs], PS[c][:, :], 1.0 / DM)
            v.tensor_scalar_mul(rB[:, cs], PS[4 + c][:, :], 1.0 / DM)
            v.tensor_tensor(tmp[:, cs], mB[:, cs], mB[:, cs], OP.mult)
            v.tensor_tensor(rB[:, cs], rB[:, cs], tmp[:, cs], OP.subtract)
            v.tensor_scalar_add(rB[:, cs], rB[:, cs], 1.0e-5)
            last = v.reciprocal(tmp[:, cs], rB[:, cs])
        return last
    v_stats = p.stage("vector", fv2, waits={"T": t_sums})

    s_rstd = p.stage("scalar", lambda s: s.activation(rB[:, :], tmp[:, :], ACT.Sqrt),
                     waits={"V": v_stats})

    def fv3(v):
        last = None
        for i in range(NDT):
            v.tensor_tensor(tmp[:, :], dt_sl(xT, i, NLOC), mB[:, :], OP.subtract)
            last = v.tensor_tensor(dt_sl(xn, i, NLOC), tmp[:, :], rB[:, :], OP.mult)
        last = v.tensor_copy(sb["lnstat"][0:8, 0:NLOC], mB[0:8, :])
        return last
    p.stage("vector", fv3, waits={"S": s_rstd, "T": t_sums})
    ar.pop()



def emit_l2_ffn(p, st):
    A, sb, PS = st["A"], st["sb"], st["PS"]
    xT, xn, ar = sb["xT"], sb["xn"], sb["arena"]

    ar.push()
    HW = 1536
    wg = ar.alloc(NDT * HW, BF16)
    wu = ar.alloc(NDT * HW, BF16)
    wd = ar.alloc(12 * DM, BF16)
    hbuf = ar.alloc(12 * 512, BF16)
    sg = ar.alloc(2 * 512, BF16)

    import os
    _dbg = int(os.environ.get('L2FFN_DBG', '99'))
    barrier = p.snap()
    prev_v = p.val("V")
    for hh in range(2 if _dbg > 3 else 1):
        def fw(g, mark, hh=hh):
            load_wT(g, mark, wg, A["l2f_gate"][:, hh * HW:(hh + 1) * HW], NDT, HW)
            load_wT(g, mark, wu, A["l2f_up"][:, hh * HW:(hh + 1) * HW], NDT, HW)
            load_wT(g, mark, wd, A["l2f_down"][hh * HW:(hh + 1) * HW, :], 12, DM)
        w0 = p.dma(fw, waits=dict(barrier, V=prev_v, T=p.val("T")), n=3)

        for c in range(4 if _dbg > 2 else (0 if _dbg == 0 else 1)):
            cs0, cs1 = c * 512, (c + 1) * 512
            rhs = [dt_sl(xn, k, NLOC, cs0, cs1) for k in range(NDT)]
            evac_marks = []
            for g in range(6 if _dbg > 1 else 1):
                bk = (g % 2) * 4
                hp0 = 2 * g
                wv = {"W": w0, "V": evac_marks[-2] if len(evac_marks) >= 2 else prev_v}

                def fmm(t, hp0=hp0, bk=bk, rhs=rhs):
                    last = None
                    for j in range(2):
                        hp = hp0 + j
                        lg = [wg[:, k * HW + hp * 128: k * HW + (hp + 1) * 128] for k in range(NDT)]
                        lu = [wu[:, k * HW + hp * 128: k * HW + (hp + 1) * 128] for k in range(NDT)]
                        mm_acc(t, PS[bk + j][:, :], lg, rhs)
                        last = mm_acc(t, PS[bk + 2 + j][:, :], lu, rhs)
                    return last
                tmark = p.stage("tensor", fmm, waits=wv)

                def fsil(s, bk=bk):
                    s.activation(sg[:, 0:512], PS[bk][:, :], ACT.Silu, scale=1.0 / 127.0)
                    return s.activation(sg[:, 512:1024], PS[bk + 1][:, :], ACT.Silu, scale=1.0 / 127.0)
                smark = p.stage("scalar", fsil,
                                waits={"T": tmark,
                                       "V": evac_marks[-1] if evac_marks else prev_v})

                def fh(v, hp0=hp0, bk=bk):
                    v.tensor_tensor(hbuf[:, hp0 * 512:(hp0 + 1) * 512], sg[:, 0:512],
                                    PS[bk + 2][:, :], OP.mult)
                    return v.tensor_tensor(hbuf[:, (hp0 + 1) * 512:(hp0 + 2) * 512], sg[:, 512:1024],
                                           PS[bk + 3][:, :], OP.mult)
                evac_marks.append(p.stage("vector", fh, waits={"S": smark}))

            def fdown(t):
                last = None
                for i in range(NDT):
                    last = mm_acc(t, PS[i][:, :],
                                  [wd[:, k * DM + i * 128: k * DM + (i + 1) * 128] for k in range(12)],
                                  [hbuf[:, k * 512:(k + 1) * 512] for k in range(12)])
                return last
            t_down = p.stage("tensor", fdown, waits={"V": evac_marks[-1]})

            def fadd(v):
                last = None
                for i in range(NDT):
                    last = v.scalar_tensor_tensor(dt_sl(xT, i, NLOC, cs0, cs1), PS[i][:, :],
                                                  1.0 / (127.0 * 127.0), dt_sl(xT, i, NLOC, cs0, cs1),
                                                  OP.mult, OP.add)
                return last
            prev_v = p.stage("vector", fadd, waits={"T": t_down})
    ar.pop()


def exchange_xn(p, st):
    A, sb, cc = st["A"], st["sb"], st["cc"]
    xn = sb["xn"]
    pre_v = p.val("V")

    def fcp(g, mark):
        for k in range(NDT):
            mark(g.dma_start(out=cc["in_big"].ap()[k * 128:(k + 1) * 128, :],
                             in_=dt_sl(xn, k, NLOC)))
    x0 = p.dma(fcp, waits={"V": pre_v}, n=NDT, sem="X")

    if SIM_NO_CC:
        def fcc2(g, mark):
            mark(g.dma_start(out=cc["out_big"].ap()[0:DM, :], in_=cc["in_big"].ap()))
            mark(g.dma_start(out=cc["out_big"].ap()[DM:2 * DM, :], in_=cc["in_big"].ap()))
        x1 = p.dma(fcc2, waits={"X": x0}, n=2, sem="X")
        return p.stage("gpsimd", lambda g: g.engine_nop(), waits={"X": x1})

    def fcc(g):
        g.wait_ge(p.sems["X"], x0)
        return g.collective_compute(
            "AllGather", OP.bypass, replica_groups=cc["groups"],
            ins=[cc["in_big"].ap().opt()], outs=[cc["out_big"].ap().opt()])
    return p.stage("gpsimd", fcc)


def load_xfk(g, mark, st, slot, k):
    ob = st["cc"]["out_big"].ap()
    for r in range(2):
        mark(g.dma_start(out=slot[:, r * NLOC:(r + 1) * NLOC],
                         in_=ob[r * DM + k * 128: r * DM + (k + 1) * 128, :]))


def emit_l3_attn_b(p, st):
    A, sb, PS = st["A"], st["sb"], st["PS"]
    xT, xn, ar, consts, constf = sb["xT"], sb["xn"], sb["arena"], sb["consts"], sb["constf"]
    ones = consts[:, 0:128]
    pre = p.snap()
    ar.push()
    wslot = ar.alloc(NDT * 512, BF16)
    qt = ar.alloc(NDT * NLOC, BF16)
    kt = ar.alloc(NDT * NLOC, BF16)
    tst = ar.alloc(8 * 128, F32)
    exrow = ar.alloc(64 * 128, BF16)
    negm = ar.alloc(64, F32)
    rZ = ar.alloc(64, F32)
    rzrow = ar.alloc(256, BF16)
    prbrow = exrow

    def fb(g, mark):
        for i, nm in enumerate(["q", "k", "v", "o"]):
            mark(g.dma_start(out=constf[:, 64 + i * 8:64 + (i + 1) * 8],
                             in_=A[f"l3b_b{nm}"].rearrange("n p one -> p (n one)"),
                             allow_slow_non_contiguous=True))
    wb = p.dma(fb, waits=pre, n=4)
    bofs = {nm: constf[:, 64 + i * 8:64 + (i + 1) * 8] for i, nm in
            enumerate(["q", "k", "v", "o"])}

    def proj(dst, wkey, bias, extra_waits=None):
        smkl = 0
        for qq in range(4):
            wm = p.dma(lambda g, mark: load_wT(
                g, mark, wslot, A[wkey][:, qq * 256:(qq + 1) * 256], NDT, 256),
                waits=dict(extra_waits or {}, T=p.val("T"), S=p.val("S")), n=1)
            for ii in range(2):
                i = qq * 2 + ii
                par = i % 2
                tm = p.stage("tensor", lambda t: [
                    mm_acc(t, PS[par * 4 + c][:, :],
                           [wslot[:, k * 256 + ii * 128: k * 256 + (ii + 1) * 128]
                            for k in range(NDT)],
                           [dt_sl(xn, k, NLOC, c * 512, (c + 1) * 512) for k in range(NDT)])
                    for c in range(4)],
                    waits={"W": wm, "V": p.val("V"), "S": max(smkl - 1, 0)})
                smkl = p.stage("scalar", lambda s: [
                    s.activation(dt_sl(dst, i, NLOC, c * 512, (c + 1) * 512),
                                 PS[par * 4 + c][:, :], ACT.Identity,
                                 bias=bias[:, i:i + 1]) for c in range(4)],
                    waits={"T": tm, "W": wb})
        return smkl

    qk0 = proj(qt, "l3b_wq", bofs["q"], pre)
    qk_done = proj(kt, "l3b_wk", bofs["k"])

    def subap(buf, h, seg):
        dtl, prow = (h * 64) // 128, (h * 64) % 128
        off = dtl * NLOC + seg * 512 + (h % 4)
        return buf[prow:prow + 64, off:off + 509:4]

    exp_marks = [0, 0]
    for g8 in range(8):
        combos = [(g8 * 8 + j) for j in range(8)]
        bk = 4 + (g8 % 2) * 2
        vq = p.stage("vector", lambda v: [
            v.tensor_tensor(tst[0:64, j * 128:(j + 1) * 128],
                            subap(qt, cm % 16, cm // 16), subap(kt, cm % 16, cm // 16),
                            OP.mult) for j, cm in enumerate(combos)],
            waits={"S": max(qk_done, exp_marks[g8 % 2]), "T": p.val("T")})
        tq = p.stage("tensor", lambda t: [
            t.matmul(PS[bk + j // 4][0:1, (j % 4) * 128:(j % 4 + 1) * 128],
                     constf[0:64, 2:3], tst[0:64, j * 128:(j + 1) * 128],
                     start=True, stop=True) for j in range(8)],
            waits={"V": vq, "S": exp_marks[g8 % 2]})
        vmx = p.stage("vector", lambda v: [
            v.reduce_max(negm[0:1, cm:cm + 1],
                         PS[bk + j // 4][0:1, (j % 4) * 128:(j % 4 + 1) * 128],
                         axis=AX, negate=True) for j, cm in enumerate(combos)],
            waits={"T": tq})
        vsc = p.stage("vector", lambda v: v.tensor_scalar_mul(
            negm[0:1, g8 * 8:(g8 + 1) * 8], negm[0:1, g8 * 8:(g8 + 1) * 8], 0.125),
            waits={"V": vmx})
        exp_marks[g8 % 2] = p.stage("scalar", lambda s: [
            s.activation(exrow[0:1, cm * 128:(cm + 1) * 128],
                         PS[bk + j // 4][0:1, (j % 4) * 128:(j % 4 + 1) * 128], ACT.Exp,
                         scale=0.125, bias=negm[0:1, cm:cm + 1])
            for j, cm in enumerate(combos)], waits={"V": vsc})

    s_all = max(exp_marks)
    v2 = p.stage("vector", lambda v: [
        v.reduce_sum(rZ[0:1, cm:cm + 1], exrow[0:1, cm * 128:(cm + 1) * 128], axis=AX)
        for cm in range(64)], waits={"S": s_all})
    v2b = p.stage("vector", lambda v: v.reciprocal(rZ[0:1, :], rZ[0:1, :]),
                  waits={"V": v2})
    vpr = v2b

    vt = qt
    ao = kt
    vt_done = proj(vt, "l3b_wv", bofs["v"], {"V": vpr})

    vz = p.stage("vector", lambda v: [
        v.memset(dt_sl(ao, i, NLOC), 0.0) for i in range(NDT)], waits={"S": vt_done})
    ap_mark = vz
    rz_mark = vpr
    for g4 in range(16):
        combos = [g4 * 4 + j for j in range(4)]
        rz_mark = p.stage("vector", lambda v: [
            v.tensor_scalar(rzrow[0:1, j * 64:(j + 1) * 64], ones[0:1, 0:64],
                            rZ[0:1, cm:cm + 1], None, OP.mult)
            for j, cm in enumerate(combos)], waits={"V": max(rz_mark, ap_mark)})
        tb = p.stage("tensor", lambda t: [
            t.matmul(PS[j % 4][0:64, 0:128], rzrow[0:1, j * 64:(j + 1) * 64],
                     prbrow[0:1, cm * 128:(cm + 1) * 128], start=True, stop=True)
            for j, cm in enumerate(combos)], waits={"V": rz_mark})
        ap_mark = p.stage("vector", lambda v: [
            v.tensor_tensor(subap(ao, cm % 16, cm // 16),
                            subap(vt, cm % 16, cm // 16), PS[j % 4][0:64, 0:128], OP.mult)
            for j, cm in enumerate(combos)], waits={"T": tb})

    prev_v = ap_mark
    for qq in range(4):
        wo0 = p.dma(lambda g, mark: load_wT(
            g, mark, wslot, A["l3b_wo"][:, qq * 256:(qq + 1) * 256], NDT, 256),
            waits={"T": p.val("T"), "V": prev_v}, n=1)
        for ii in range(2):
            i = qq * 2 + ii
            par = i % 2
            t_o = p.stage("tensor", lambda t: [
                mm_acc(t, PS[par * 4 + c][:, :],
                       [wslot[:, k * 256 + ii * 128: k * 256 + (ii + 1) * 128]
                        for k in range(NDT)],
                       [dt_sl(ao, k, NLOC, c * 512, (c + 1) * 512) for k in range(NDT)])
                for c in range(4)], waits={"V": prev_v, "W": wo0})
            prev_v = p.stage("vector", lambda v: [
                (v.tensor_tensor(dt_sl(xT, i, NLOC, c * 512, (c + 1) * 512),
                                 dt_sl(xT, i, NLOC, c * 512, (c + 1) * 512),
                                 PS[par * 4 + c][:, :], OP.add),
                 v.tensor_scalar_add(dt_sl(xT, i, NLOC, c * 512, (c + 1) * 512),
                                     dt_sl(xT, i, NLOC, c * 512, (c + 1) * 512),
                                     bofs["o"][:, i:i + 1]))[-1] for c in range(4)],
                waits={"T": t_o})
    ar.pop()

def emit_l0_ffn(p, st):
    A, sb, PS = st["A"], st["sb"], st["PS"]
    xT, xn, ar, consts = sb["xT"], sb["xn"], sb["arena"], sb["consts"]
    eye = consts[:, 128:256]
    rowsel = consts[:, 384:1408]
    ar.push()
    wrtf = ar.alloc(NDT * 8, F32)
    ltf = ar.alloc(NLOC, F32)
    scol = ar.alloc(1, F32)
    L = ar.alloc(16 * 8, F32)
    negm = ar.alloc(16, F32)
    ohb = ar.alloc(16 * 8, BF16)
    ohT = ar.alloc(NLOC, BF16)
    fce = ar.alloc(NDT * 512, BF16)
    pje = ar.alloc(4 * DM, BF16)
    xm = ar.alloc(NDT * 512, BF16)
    ab = ar.alloc(4 * 512, F32)
    lin = ar.alloc(4 * 512, F32)
    he = ar.alloc(4 * 512, BF16)
    pre = p.snap()

    w0 = p.dma(lambda g, mark: load_wT(g, mark, wrtf, A["l0f_router"], NDT, 8),
               waits=pre, n=1)
    constf = sb["constf"]
    lnstat = sb["lnstat"]

    t1 = p.stage("tensor", lambda t: [
        mm_acc(t, PS[c][0:8, :], [wrtf[:, k * 8:(k + 1) * 8] for k in range(NDT)],
               [dt_sl(xT, k, NLOC, c * 512, (c + 1) * 512) for k in range(NDT)])
        for c in range(4)] + [
        mm_acc(t, PS[4][0:8, 0:1], [wrtf[:, k * 8:(k + 1) * 8] for k in range(NDT)],
               [constf[:, 2:3]] * NDT)], waits={"W": w0, "V": p.val("V")})
    v0 = p.stage("vector", lambda v: v.tensor_copy(scol[0:8, :], PS[4][0:8, 0:1]),
                 waits={"T": t1})
    v0b = p.stage("vector", lambda v: v.tensor_scalar(
        ltf[32:40, :], lnstat[0:8, 0:NLOC], scol[0:8, 0:1], None, OP.mult),
        waits={"V": v0})
    v1 = p.stage("vector", lambda v: [
        v.tensor_tensor(ltf[0:8, c * 512:(c + 1) * 512], PS[c][0:8, :],
                        ltf[32:40, c * 512:(c + 1) * 512], OP.subtract)
        for c in range(4)], waits={"V": v0b})
    t2 = p.stage("tensor", lambda t: [
        t.transpose(PS[4][:, ti * 8:(ti + 1) * 8],
                    ltf[0:8, ti * 128:(ti + 1) * 128], constf[0:8, 32:40])
        for ti in range(16)], waits={"V": v1})
    v2 = p.stage("vector", lambda v:
                 v.tensor_copy(L[:, :], PS[4][:, 0:128]), waits={"T": t2})
    v3 = p.stage("vector", lambda v: [
        v.reduce_max(negm[:, ti:ti + 1], L[:, ti * 8:(ti + 1) * 8], axis=AX, negate=True)
        for ti in range(16)], waits={"V": v2})
    v4 = p.stage("vector", lambda v: [
        v.tensor_scalar(ohb[:, ti * 8:(ti + 1) * 8], L[:, ti * 8:(ti + 1) * 8],
                        negm[:, ti:ti + 1], 0.0, OP.add, OP.is_ge) for ti in range(16)],
        waits={"V": v3})
    t3 = p.stage("tensor", lambda t: [
        t.transpose(PS[5 + ti // 8][:, (ti % 8) * 64:(ti % 8) * 64 + 64].bitcast(BF16)[0:8, :],
                    ohb[:, ti * 8:(ti + 1) * 8], eye[:, 0:128]) for ti in range(16)],
        waits={"V": v4})
    v5 = p.stage("vector", lambda v: [
        v.tensor_copy(ohT[0:8, ti * 128:(ti + 1) * 128],
                      PS[5 + ti // 8][:, (ti % 8) * 64:(ti % 8) * 64 + 64].bitcast(BF16)[0:8, :])
        for ti in range(16)], waits={"T": t3})

    prev_t = p.val("T")
    prev_v = p.val("V")
    for e in range(8):
        def fwE(g, mark):
            load_wT(g, mark, fce, A["l0f_fc"][e], NDT, 512)
            load_wT(g, mark, pje, A["l0f_proj"][e], 4, DM)
        wE = p.dma(fwE, waits={"T": prev_t, "V": prev_v, "W": p.val("W")}, n=2)
        for c in range(4):
            cs0, cs1 = c * 512, (c + 1) * 512
            tb = p.stage("tensor", lambda t: t.matmul(
                PS[0][:, :], rowsel[0:8, e * 128:(e + 1) * 128], ohT[0:8, cs0:cs1],
                start=True, stop=True), waits={"V": max(v5, prev_v), "W": wE})
            vm = p.stage("vector", lambda v: [
                v.tensor_tensor(dt_sl(xm, k, 512), dt_sl(xn, k, NLOC, cs0, cs1),
                                PS[0][:, :], OP.mult) for k in range(NDT)],
                waits={"T": tb})
            th = p.stage("tensor", lambda t: [
                mm_acc(t, PS[1 + hp][:, :],
                       [fce[:, k * 512 + hp * 128:k * 512 + (hp + 1) * 128] for k in range(NDT)],
                       [dt_sl(xm, k, 512) for k in range(NDT)]) for hp in range(4)],
                waits={"V": vm})
            sa = p.stage("scalar", lambda s: [
                s.activation(ab[:, hp * 512:(hp + 1) * 512], PS[1 + hp][:, :], ACT.Abs)
                for hp in range(4)], waits={"T": th})
            vl = p.stage("vector", lambda v: [
                v.scalar_tensor_tensor(lin[:, hp * 512:(hp + 1) * 512],
                                       ab[:, hp * 512:(hp + 1) * 512], 1.0 / 3.0,
                                       PS[1 + hp][:, :], OP.mult, OP.add)
                for hp in range(4)], waits={"S": sa})
            sq = p.stage("scalar", lambda s: [
                s.activation(he[:, hp * 512:(hp + 1) * 512], lin[:, hp * 512:(hp + 1) * 512],
                             ACT.Square, scale=0.75) for hp in range(4)],
                waits={"V": vl})
            for wv in range(2):
                t_o = p.stage("tensor", lambda t: [
                    mm_acc(t, PS[4 + i][:, :],
                           [pje[:, k * DM + (wv * 4 + i) * 128: k * DM + (wv * 4 + i + 1) * 128]
                            for k in range(4)],
                           [he[:, k * 512:(k + 1) * 512] for k in range(4)])
                    for i in range(4)], waits={"S": sq, "V": prev_v})
                prev_v = p.stage("vector", lambda v: [
                    v.tensor_tensor(dt_sl(xT, wv * 4 + i, NLOC, cs0, cs1),
                                    dt_sl(xT, wv * 4 + i, NLOC, cs0, cs1),
                                    PS[4 + i][:, :], OP.add) for i in range(4)],
                    waits={"T": t_o})
                prev_t = t_o
    ar.pop()
def emit_softmax_attn(p, st, cfg):
    A, sb, PS = st["A"], st["sb"], st["PS"]
    xT, xn, ar, consts, constf = sb["xT"], sb["xn"], sb["arena"], sb["consts"], sb["constf"]
    ones = consts[:, 0:128]
    onesrowf = constf[0:1, 160:288]
    NHG = cfg["nh_group"]
    GW = 64 * NHG
    NGRP = 16 // NHG

    cc_done = exchange_xn(p, st)

    ar.push()
    xfk = ar.alloc(SFULL, BF16)
    wsl = ar.alloc(3 * NDT * GW, BF16)
    wosl = ar.alloc(DM, BF16) if cfg.get("wo") else None
    kt = ar.alloc(SFULL, BF16)
    vnat = ar.alloc(32 * 65 * NHG, BF16)
    qt = ar.alloc(NLOC, BF16)
    qsq = ar.alloc(NLOC, BF16)
    ksqc = ar.alloc(512, BF16)
    U = ar.alloc(NHG * 512, F32)
    sc = ar.alloc(4 * 512, F32)
    pt = ar.alloc(4 * 512, BF16)
    osb = ar.alloc(NHG * 512, BF16)
    aon = ar.alloc(512, BF16) if cfg.get("wo") else None
    kmx = ar.alloc(NHG, F32)
    nr = ar.alloc(NHG * 512, F32)
    krow = ar.alloc(24, F32)
    bb = {}
    if cfg.get("biases"):
        bbuf = ar.alloc(4 * NDT, F32)
        keys = cfg["biases"]
        def fbi(g, mark):
            for i, nm in enumerate(["q", "k", "v", "o"]):
                if keys.get(nm):
                    mark(g.dma_start(out=bbuf[:, i * NDT:(i + 1) * NDT],
                                     in_=A[keys[nm]].rearrange("n p one -> p (n one)"),
                                     allow_slow_non_contiguous=True))
        p.dma(fbi, waits=p.snap("V", "S", "T"), n=len([k for k in keys.values() if k]))
        for i, nm in enumerate(["q", "k", "v", "o"]):
            if keys.get(nm):
                bb[nm] = bbuf[:, i * NDT:(i + 1) * NDT]

    extra = cfg["setup"](p, st, ar) if cfg.get("setup") else {}

    def bslice(nm, grp):
        if nm not in bb:
            return None
        if GW == 128:
            return bb[nm][:, grp:grp + 1]
        r0 = (grp % 2) * 64
        return bb[nm][r0:r0 + 64, grp // 2:grp // 2 + 1]

    prev_v = p.val("V")
    prev_s = p.val("S")
    for grp in range(NGRP):
        per_grp = cfg.get("per_group")
        gext = per_grp(p, st, ar, extra, grp) if per_grp else None

        def fwg(g, mark):
            mark(g.dma_start(out=wsl[:, 0:NDT * GW].rearrange("p (n m) -> p n m", n=NDT),
                             in_=cfg["wq"](grp).rearrange("(n p) m -> p n m", p=128)))
            mark(g.dma_start(out=wsl[:, NDT * GW:2 * NDT * GW].rearrange("p (n m) -> p n m", n=NDT),
                             in_=cfg["wk"](grp).rearrange("(n p) m -> p n m", p=128)))
            mark(g.dma_start(out=wsl[:, 2 * NDT * GW:3 * NDT * GW].rearrange("p (n m) -> p n m", n=NDT),
                             in_=cfg["wv"](grp).rearrange("(n p) m -> p n m", p=128)))
            if cfg.get("wo"):
                mark(g.dma_start(out=wosl[0:GW, :], in_=cfg["wo"](grp)))
        wgm = p.dma(fwg, waits={"T": p.val("T"), "V": prev_v, "S": prev_s},
                    n=4 if cfg.get("wo") else 3)

        tk = 0
        for k in range(NDT):
            xm = p.dma(lambda g, mark, k=k: load_xfk(g, mark, st, xfk, k),
                       waits={"G": cc_done, "T": tk, "V": prev_v, "S": prev_s},
                       n=2, sem="X")
            tk = p.stage("tensor", lambda t, k=k: [
                t.matmul(PS[cc][0:GW, :], wsl[:, NDT * GW + k * GW:NDT * GW + (k + 1) * GW],
                         xfk[:, cc * 512:(cc + 1) * 512],
                         start=(k == 0), stop=(k == NDT - 1)) for cc in range(8)],
                waits={"X": xm, "W": wgm})
        bk_ap = bslice("k", grp)
        sk = p.stage("scalar", lambda s: [
            s.activation(kt[0:GW, cc * 512:(cc + 1) * 512], PS[cc][0:GW, :], ACT.Identity,
                         bias=(bk_ap if bk_ap is not None else 0.0))
            for cc in range(8)], waits={"T": tk})
        vmx = 0
        for h in range(NHG):
            for cc in range(8):
                ssq = p.stage("scalar", lambda s, h=h, cc=cc: s.activation(
                    ksqc[0:64, :], kt[h * 64:h * 64 + 64, cc * 512:(cc + 1) * 512],
                    ACT.Square), waits={"S": sk, "V": vmx})
                tr = p.stage("tensor", lambda t, cc=cc: t.matmul(
                    st["PS2"][cc % 2][0:1, :],
                    ones[0:64, 0:1], ksqc[0:64, :], start=True, stop=True),
                    waits={"S": ssq})
                vmx = p.stage("vector", lambda v, h=h, cc=cc: v.reduce_max(
                    krow[0:1, h * 8 + cc:h * 8 + cc + 1], st["PS2"][cc % 2][0:1, :],
                    axis=AX), waits={"T": tr})
        vmx2 = p.stage("vector", lambda v: [
            v.reduce_max(krow[0:1, 8 * NHG + h:8 * NHG + h + 1],
                         krow[0:1, h * 8:(h + 1) * 8], axis=AX) for h in range(NHG)],
            waits={"V": vmx})
        tkb = p.stage("tensor", lambda t: [
            t.matmul(st["PS2"][h][0:128, 0:1], onesrowf,
                     krow[0:1, 8 * NHG + h:8 * NHG + h + 1], start=True, stop=True)
            for h in range(NHG)], waits={"V": vmx2})
        vkm = p.stage("vector", lambda v: [
            v.tensor_copy(kmx[:, h:h + 1], st["PS2"][h][0:128, 0:1])
            for h in range(NHG)], waits={"T": tkb})

        vm = 0
        for w in range(4):
            tv = 0
            for k in range(NDT):
                def fxq(g, mark, k=k, w=w):
                    ob = st["cc"]["out_big"].ap()
                    r = w // 2
                    mark(g.dma_start(
                        out=xfk[:, 0:1024] if k % 2 == 0 else xfk[:, 1024:2048],
                        in_=ob[r * DM + k * 128: r * DM + (k + 1) * 128,
                               (w % 2) * 1024:(w % 2 + 1) * 1024]))
                xm = p.dma(fxq, waits={"G": cc_done, "T": tv,
                                       "V": vkm if (k == 0 and w == 0) else vm,
                                       "S": sk if (k == 0 and w == 0) else 0},
                           n=1, sem="X")
                tv = p.stage("tensor", lambda t, k=k: [
                    t.matmul(PS[b][:, 0:GW],
                             xfk[:, (k % 2) * 1024 + b * 128:(k % 2) * 1024 + (b + 1) * 128],
                             wsl[:, 2 * NDT * GW + k * GW:2 * NDT * GW + (k + 1) * GW],
                             start=(k == 0), stop=(k == NDT - 1)) for b in range(8)],
                    waits={"X": xm})
            def fve(v, w=w):
                last = None
                for b in range(8):
                    stt = w * 8 + b
                    for h in range(NHG):
                        last = v.tensor_copy(
                            vnat[:, stt * 65 * NHG + h * 65: stt * 65 * NHG + h * 65 + 64],
                            PS[b][:, h * 64:h * 64 + 64])
                return last
            vm = p.stage("vector", fve, waits={"T": tv})
        vm = p.stage("vector", lambda v: v.memset(
            bass.AP(vnat.tensor, vnat.offset + 64,
                    [[vnat.ap[0][0], 128], [65, 32 * NHG]]), 1.0), waits={"V": vm})

        tq = p.stage("tensor", lambda t: [
            mm_acc(t, PS[c][0:GW, :],
                   [wsl[:, k * GW:(k + 1) * GW] for k in range(NDT)],
                   [dt_sl(xn, k, NLOC, c * 512, (c + 1) * 512) for k in range(NDT)])
            for c in range(4)], waits={"V": vm})
        bq_ap = bslice("q", grp)
        sq1 = p.stage("scalar", lambda s: [
            s.activation(qt[0:GW, c * 512:(c + 1) * 512], PS[c][0:GW, :], ACT.Identity,
                         bias=(bq_ap if bq_ap is not None else 0.0))
            for c in range(4)], waits={"T": tq})
        sq2 = p.stage("scalar", lambda s: [
            s.activation(qsq[0:GW, c * 512:(c + 1) * 512], PS[c][0:GW, :], ACT.Square,
                         bias=(bq_ap if bq_ap is not None else 0.0))
            for c in range(4)], waits={"T": tq})

        ux = cfg["u_extra"](p, st, ar, extra, gext, grp) if cfg.get("u_extra") else None
        bv_ap = bslice("v", grp)

        exp_marks = {}
        out_marks = prev_v
        for qc in range(4):
            q0, q1 = qc * 512, (qc + 1) * 512
            tub = p.stage("tensor", lambda t: [
                t.matmul(st["PS2"][h][:, :], ones[h * 64:(h + 1) * 64, 0:128],
                         qsq[h * 64:h * 64 + 64, q0:q1], start=True, stop=True)
                for h in range(NHG)],
                waits={"S": sq2, "V": out_marks})
            su = p.stage("scalar", lambda s: [
                s.activation(U[:, h * 512:(h + 1) * 512], st["PS2"][h][:, :], ACT.Sqrt,
                             scale=kmx[:, h:h + 1]) for h in range(NHG)],
                waits={"T": tub, "V": vkm})
            if ux is not None:
                su = p.stage("vector", lambda v: [
                    v.tensor_scalar_add(U[:, h * 512:(h + 1) * 512],
                                        U[:, h * 512:(h + 1) * 512], ux[0])
                    for h in range(NHG)], waits={"S": su, **({"G": ux[1]} if ux[1] else {})})
                su_sem = "V"
            else:
                su_sem = "S"

            for stt in range(33):
                def ft(t, stt=stt):
                    last = None
                    if stt > 0:
                        sp = stt - 1
                        for h in range(NHG):
                            r0 = ((grp * GW) + h * 64) % 128
                            ptile = pt[:, (2 * h + sp % 2) * 512:(2 * h + sp % 2 + 1) * 512]
                            last = t.matmul(
                                PS[4 + h][r0:r0 + 64, :],
                                vnat[:, sp * 65 * NHG + h * 65: sp * 65 * NHG + h * 65 + 64],
                                ptile, start=(sp == 0), stop=(sp == 31))
                            last = t.matmul(
                                st["PS2"][h][0:1, :],
                                vnat[:, sp * 65 * NHG + h * 65 + 64: sp * 65 * NHG + h * 65 + 65],
                                ptile, start=(sp == 0), stop=(sp == 31))
                    if stt < 32:
                        for h in range(NHG):
                            last = t.matmul(
                                PS[2 * h + stt % 2][:, :],
                                kt[h * 64:h * 64 + 64, stt * 128:(stt + 1) * 128],
                                qt[h * 64:h * 64 + 64, q0:q1], start=True, stop=True)
                    return last
                wv_t = {"S": exp_marks.get(stt - 1, 0)}
                if stt == 0:
                    wv_t[su_sem] = max(wv_t.get(su_sem, 0), su)
                if stt >= 2:
                    wv_t["S"] = max(wv_t["S"], exp_marks.get(stt - 2, 0))
                tsc = p.stage("tensor", ft, waits=wv_t)
                if stt >= 32:
                    break
                def fsub(v, stt=stt):
                    last = None
                    for h in range(NHG):
                        slot = sc[:, (2 * h + stt % 2) * 512:(2 * h + stt % 2 + 1) * 512]
                        last = v.tensor_tensor(slot, PS[2 * h + stt % 2][:, :],
                                               U[:, h * 512:(h + 1) * 512], OP.subtract)
                        if cfg.get("bias_ops"):
                            last = cfg["bias_ops"](v, extra, gext, grp, h, qc, stt, slot) or last
                    return last
                vsub = p.stage("vector", fsub, waits={"T": tsc})
                exp_marks[stt] = p.stage("scalar", lambda s, stt=stt: [
                    s.activation(pt[:, (2 * h + stt % 2) * 512:(2 * h + stt % 2 + 1) * 512],
                                 sc[:, (2 * h + stt % 2) * 512:(2 * h + stt % 2 + 1) * 512],
                                 ACT.Exp, scale=0.125) for h in range(NHG)],
                    waits={"V": vsub})
            av_done = tsc

            vrz = p.stage("vector", lambda v: [
                v.reciprocal(nr[0:1, h * 512:(h + 1) * 512], st["PS2"][h][0:1, :])
                for h in range(NHG)], waits={"T": av_done})
            def fsev(s):
                last = None
                for h in range(NHG):
                    r0 = ((grp * GW) + h * 64) % 128
                    last = s.activation(osb[r0:r0 + 64, h % 2 * 512:(h % 2) * 512 + 512],
                                        PS[4 + h][r0:r0 + 64, :], ACT.Identity)
                return last
            sev = p.stage("scalar", fsev, waits={"T": av_done})
            trz = p.stage("tensor", lambda t: [
                t.matmul(st["PS2"][h][:, :], onesrowf, nr[0:1, h * 512:(h + 1) * 512],
                         start=True, stop=True) for h in range(NHG)],
                waits={"V": vrz})
            if not cfg.get("wo"):
                def fmul(v):
                    last = None
                    for h in range(NHG):
                        r0 = ((grp * GW) + h * 64) % 128
                        last = v.tensor_tensor(sc[r0:r0 + 64, h % 2 * 512:(h % 2) * 512 + 512],
                                               osb[r0:r0 + 64, h % 2 * 512:(h % 2) * 512 + 512],
                                               st["PS2"][h][r0:r0 + 64, :], OP.mult)
                    return last
                vmul = p.stage("vector", fmul, waits={"T": trz, "S": sev})
                def fadd(v):
                    last = None
                    for h in range(NHG):
                        r0 = ((grp * GW) + h * 64) % 128
                        dtl = (grp * GW + h * 64) // 128
                        dst = xT[r0:r0 + 64, dtl * NLOC + q0: dtl * NLOC + q1]
                        last = v.tensor_tensor(dst, dst,
                                               sc[r0:r0 + 64, h % 2 * 512:(h % 2) * 512 + 512],
                                               OP.add)
                        if bv_ap is not None:
                            bcol = bv_ap[h * 64:h * 64 + 64, :] if GW == 128 else bv_ap
                            last = v.tensor_scalar_add(dst, dst, bcol)
                    return last
                out_marks = p.stage("vector", fadd, waits={"V": vmul})
            else:
                def fmul(v):
                    last = None
                    for h in range(NHG):
                        r0 = h * 64
                        last = v.tensor_tensor(aon[r0:r0 + 64, :],
                                               osb[r0:r0 + 64, h % 2 * 512:(h % 2) * 512 + 512],
                                               st["PS2"][h][r0:r0 + 64, :], OP.mult)
                    return last
                vmul = p.stage("vector", fmul, waits={"T": trz, "S": sev})
                if bv_ap is not None:
                    vmul = p.stage("vector", lambda v: [
                        v.tensor_scalar_add(aon[h * 64:h * 64 + 64, :],
                                            aon[h * 64:h * 64 + 64, :],
                                            bv_ap[h * 64:h * 64 + 64, :] if GW == 128 else bv_ap)
                        for h in range(NHG)], waits={"V": vmul})
                for wv2 in range(2):
                    t_o = p.stage("tensor", lambda t: [
                        t.matmul(PS[i][:, :], wosl[0:GW, (wv2 * 4 + i) * 128:(wv2 * 4 + i + 1) * 128],
                                 aon[0:GW, :], start=True, stop=True) for i in range(4)],
                        waits={"V": vmul if wv2 == 0 else out_marks})
                    out_marks = p.stage("vector", lambda v: [
                        v.tensor_tensor(dt_sl(xT, wv2 * 4 + i, NLOC, q0, q1),
                                        dt_sl(xT, wv2 * 4 + i, NLOC, q0, q1),
                                        PS[i][:, :], OP.add) for i in range(4)],
                        waits={"T": t_o})
        prev_v = out_marks
        prev_s = max(exp_marks.values())

    if cfg.get("wo") and cfg.get("biases", {}).get("o"):
        bo = bb["o"]
        prev_v = p.stage("vector", lambda v: [
            v.tensor_scalar_add(dt_sl(xT, i, NLOC), dt_sl(xT, i, NLOC), bo[:, i:i + 1])
            for i in range(NDT)], waits={"V": prev_v})
    ar.pop()


def emit_l0_attn_b(p, st):
    emit_softmax_attn(p, st, dict(
        nh_group=2,
        wq=lambda grp: st["A"]["l0b_wqkv"][:, grp * 128:(grp + 1) * 128],
        wk=lambda grp: st["A"]["l0b_wqkv"][:, 1024 + grp * 128:1024 + (grp + 1) * 128],
        wv=lambda grp: st["A"]["l0b_wqkv"][:, 2048 + grp * 128:2048 + (grp + 1) * 128],
    ))
def emit_l1_attn_a(p, st):
    A, sb, PS = st["A"], st["sb"], st["PS"]
    xT, xn, ar, consts, constf = sb["xT"], sb["xn"], sb["arena"], sb["consts"], sb["constf"]
    ones = consts[:, 0:128]
    eye = consts[:, 128:256]
    PS2 = st["PS2"]
    pre = p.snap()
    ar.push()
    stash = ar.alloc(1024, F32)
    sful = ar.alloc(1024, F32)
    sfulb = ar.alloc(1024, BF16)
    srep = ar.alloc(1024, BF16)
    bd = ar.alloc(1024, BF16)
    repm = ar.alloc(128, BF16)
    expm = ar.alloc(NDT * 128, BF16)
    rowstat = ar.alloc(256, F32)
    colstat = ar.alloc(48, F32)
    colb = ar.alloc(16, BF16)
    bqc = ar.alloc(24, F32)
    gnwc = ar.alloc(16, F32)

    def fb(g, mark):
        mark(g.dma_start(out=bqc[:, :], in_=A["l1a_bqkv"].rearrange("n p one -> p (n one)"),
                         allow_slow_non_contiguous=True))
        mark(g.dma_start(out=repm[0:32, :], in_=A["repmat32"]))
        mark(g.dma_start(out=expm[0:32, :], in_=A["l1a_exp"]))
        mark(g.dma_start(out=gnwc[:, 0:8], in_=A["l1a_gnw"].rearrange("n p one -> p (n one)"),
                         allow_slow_non_contiguous=True))
        mark(g.dma_start(out=gnwc[:, 8:16], in_=A["l1a_gnb"].rearrange("n p one -> p (n one)"),
                         allow_slow_non_contiguous=True))
    wb = p.dma(fb, waits=pre, n=5)

    ar.push()
    wslot = ar.alloc(NDT * 512, BF16)
    knat = ar.alloc(16 * 512, BF16)
    vnat = ar.alloc(16 * 512, BF16)
    bB = ar.alloc(512, BF16)
    prev_v = p.val("V")
    vstm = prev_v
    for dhalf in range(2):
        for j, (dst, wcol, brow) in enumerate([(knat, 1024, "l1a_kb_row"),
                                               (vnat, 2048, "l1a_vb_row")]):
            wm = p.dma(lambda g, mark, wcol=wcol, dhalf=dhalf: load_wT(
                g, mark, wslot,
                A["l1a_wqkv"][:, wcol + dhalf * 512:wcol + dhalf * 512 + 512], NDT, 512),
                waits={"T": p.val("T"), "V": prev_v, "W": wb}, n=1)
            bm = p.dma(lambda g, mark, brow=brow, dhalf=dhalf: mark(g.dma_start(
                out=bB[0:1, :], in_=A[brow][:, dhalf * 512:(dhalf + 1) * 512])),
                waits={"V": prev_v}, n=1)
            tb = p.stage("tensor", lambda t: t.matmul(
                PS2[0][:, :], ones[0:1, 0:128], bB[0:1, :], start=True, stop=True),
                waits={"W": bm, "V": prev_v})
            vb = p.stage("vector", lambda v: v.tensor_copy(bB[:, :], PS2[0][:, :]),
                         waits={"T": tb})
            vm = vb
            for w in range(2):
                tv = 0
                for k in range(NDT):
                    tv = p.stage("tensor", lambda t, k=k, w=w: [
                        t.matmul(PS[b][:, :],
                                 dt_sl(xn, k, NLOC, (w * 8 + b) * 128, (w * 8 + b + 1) * 128),
                                 wslot[:, k * 512:(k + 1) * 512],
                                 start=(k == 0), stop=(k == NDT - 1)) for b in range(8)],
                        waits={"V": vm, "W": wm} if k == 0 else {})
                vm = p.stage("vector", lambda v, w=w, dst=dst: [
                    v.tensor_copy(dst[:, (w * 8 + b) * 512:(w * 8 + b + 1) * 512],
                                  PS[b][:, :]) for b in range(8)], waits={"T": tv})
            prev_v = p.stage("vector", lambda v, dst=dst: [
                v.tensor_tensor(dst[:, s2 * 512:(s2 + 1) * 512],
                                dst[:, s2 * 512:(s2 + 1) * 512], bB[:, :], OP.add)
                for s2 in range(16)], waits={"V": vm})
        vstm = prev_v
        vmarks = {}
        for hh in range(16):
            h = dhalf * 16 + hh
            th = p.stage("tensor", lambda t, hh=hh: mm_acc(
                t, PS[hh % 2][0:32, 0:32],
                [knat[:, s2 * 512 + hh * 32:s2 * 512 + (hh + 1) * 32] for s2 in range(16)],
                [vnat[:, s2 * 512 + hh * 32:s2 * 512 + (hh + 1) * 32] for s2 in range(16)]),
                waits={"V": vmarks.get(hh - 2, vstm)})
            vmarks[hh] = p.stage("vector", lambda v, h=h, hh=hh: v.tensor_copy(
                stash[0:32, h * 32:(h + 1) * 32], PS[hh % 2][0:32, 0:32]),
                waits={"T": th})
        vstm = vmarks[15]
        prev_v = vstm
    ar.pop()

    gx = exchange_small(p, st, stash[0:32, 0:1024], 32, 1024, {"V": vstm})
    mm1 = p.dma(lambda g, mark: [
        mark(g.dma_start(out=sful[0:32, :], in_=st["cc"]["out_small"].ap()[0:32, 0:1024])),
        mark(g.dma_start(out=stash[0:32, :], in_=st["cc"]["out_small"].ap()[128:160, 0:1024]))],
        waits={"G": gx}, n=2, sem="X")
    v1 = p.stage("vector", lambda v: v.tensor_tensor(sful[0:32, :], sful[0:32, :],
                                                     stash[0:32, :], OP.add),
                 waits={"X": mm1})
    v1b = p.stage("vector", lambda v: v.tensor_copy(sfulb[0:32, :], sful[0:32, :]),
                  waits={"V": v1})
    t2 = p.stage("tensor", lambda t: [
        t.matmul(PS[cc][:, :], repm[0:32, :], sfulb[0:32, cc * 512:(cc + 1) * 512],
                 start=True, stop=True) for cc in range(2)], waits={"V": v1b})
    v2 = p.stage("vector", lambda v: [
        v.tensor_copy(srep[:, cc * 512:(cc + 1) * 512], PS[cc][:, :]) for cc in range(2)],
        waits={"T": t2})
    v3 = p.stage("vector", lambda v: v.memset(bd[:, :], 0.0), waits={"V": v2})
    v4 = p.stage("vector", lambda v: [
        v.tensor_copy(bd[(j % 2) * 64 + sub * 32:(j % 2) * 64 + sub * 32 + 32,
                         j * 64 + sub * 32:j * 64 + sub * 32 + 32],
                      srep[(j % 2) * 64 + sub * 32:(j % 2) * 64 + sub * 32 + 32,
                           (2 * j + sub) * 32:(2 * j + sub + 1) * 32])
        for j in range(16) for sub in range(2)], waits={"V": v3})

    ar.push()
    wslot = ar.alloc(NDT * 1024, BF16)
    ctx = ar.alloc(NDT * NLOC, BF16)
    qtd = ar.alloc(NLOC, BF16)
    sqs = ar.alloc(NLOC, BF16)
    nsc = ar.alloc(NLOC, F32)
    wq0 = p.dma(lambda g, mark: load_wT(g, mark, wslot, A["l1a_wqkv"][:, 0:1024], NDT, 1024),
                waits={"T": p.val("T"), "V": v4, "W": p.val("W")}, n=1)
    vstat = v4
    sq = 0
    for d in range(NDT):
        tq = p.stage("tensor", lambda t, d=d: [
            mm_acc(t, PS[c][:, :],
                   [wslot[:, k * 1024 + d * 128:k * 1024 + (d + 1) * 128] for k in range(NDT)],
                   [dt_sl(xn, k, NLOC, c * 512, (c + 1) * 512) for k in range(NDT)])
            for c in range(4)], waits={"W": wq0, "V": vstat, "S": sq})
        sq = p.stage("scalar", lambda s, d=d: [
            s.activation(qtd[:, c * 512:(c + 1) * 512], PS[c][:, :], ACT.Identity,
                         bias=bqc[:, d:d + 1]) for c in range(4)], waits={"T": tq})
        tcx = p.stage("tensor", lambda t, d=d: [
            t.matmul(PS[4 + c][jj * 64:jj * 64 + 64, 0:512],
                     bd[jj * 64:jj * 64 + 64, (2 * d + jj) * 64:(2 * d + jj + 1) * 64],
                     qtd[jj * 64:jj * 64 + 64, c * 512:(c + 1) * 512],
                     start=True, stop=True)
            for c in range(4) for jj in range(2)], waits={"S": sq})
        vcx = p.stage("vector", lambda v, d=d: [
            v.tensor_copy(dt_sl(ctx, d, NLOC, c * 512, (c + 1) * 512), PS[4 + c][:, :])
            for c in range(4)], waits={"T": tcx})
        scq = p.stage("scalar", lambda s, d=d: s.activation(
            sqs[:, :], dt_sl(ctx, d, NLOC), ACT.Square), waits={"V": vcx})
        vstat = p.stage("vector", lambda v, d=d: [
            v.reduce_sum(colstat[:, d:d + 1], dt_sl(ctx, d, NLOC), axis=AX),
            v.reduce_sum(colstat[:, 8 + d:8 + d + 1], sqs[:, :], axis=AX)][-1],
            waits={"S": scq})
    vcb = p.stage("vector", lambda v: v.tensor_copy(colb[:, 0:16], colstat[:, 0:16]),
                  waits={"V": vstat})
    t3 = p.stage("tensor", lambda t: [
        t.transpose(PS2[d % 2][:, (d // 2) * 64:(d // 2) * 64 + 64].bitcast(BF16)[0:1, :],
                    colb[:, d:d + 1], eye) for d in range(16)], waits={"V": vcb})
    v5 = p.stage("vector", lambda v: [
        v.reduce_sum(rowstat[0:1, (d % 8) * 4 + q + (d // 8) * 32:
                             (d % 8) * 4 + q + (d // 8) * 32 + 1],
                     PS2[d % 2][:, (d // 2) * 64:(d // 2) * 64 + 64]
                     .bitcast(BF16)[0:1, q * 32:(q + 1) * 32], axis=AX)
        for d in range(16) for q in range(4)], waits={"T": t3})
    gx2 = exchange_small(p, st, rowstat[0:1, 0:64], 1, 64, {"V": v5})
    mm2 = p.dma(lambda g, mark: [
        mark(g.dma_start(out=rowstat[0:1, 64:128],
                         in_=st["cc"]["out_small"].ap()[0:1, 0:64])),
        mark(g.dma_start(out=rowstat[0:1, 128:192],
                         in_=st["cc"]["out_small"].ap()[128:129, 0:64]))],
        waits={"G": gx2}, n=2, sem="X")
    v6 = p.stage("vector", lambda v: v.tensor_tensor(
        rowstat[0:1, 64:128], rowstat[0:1, 64:128], rowstat[0:1, 128:192], OP.add),
        waits={"X": mm2})
    v7 = p.stage("vector", lambda v: v.tensor_scalar_mul(
        rowstat[0:1, 192:256], rowstat[0:1, 64:128], 1.0 / (32.0 * 4096.0)),
        waits={"V": v6})
    v8 = p.stage("vector", lambda v: [
        v.tensor_tensor(nsc[0:1, 0:32], rowstat[0:1, 192:224], rowstat[0:1, 192:224],
                        OP.mult),
        v.tensor_tensor(rowstat[0:1, 224:256], rowstat[0:1, 224:256], nsc[0:1, 0:32],
                        OP.subtract)][-1], waits={"V": v7})
    v9 = p.stage("vector", lambda v: v.tensor_scalar_add(
        rowstat[0:1, 224:256], rowstat[0:1, 224:256], 1.0e-5), waits={"V": v8})
    v10 = p.stage("vector", lambda v: v.reciprocal(rowstat[0:1, 224:256],
                                                   rowstat[0:1, 224:256]),
                  waits={"V": v9})
    s10 = p.stage("scalar", lambda s: s.activation(rowstat[0:1, 224:256],
                                                   rowstat[0:1, 224:256], ACT.Sqrt),
                  waits={"V": v10})
    v11 = p.stage("vector", lambda v: v.tensor_copy(
        nsc[0:1, 0:64].bitcast(BF16)[0:1, 0:64], rowstat[0:1, 192:256]), waits={"S": s10})
    t4 = p.stage("tensor", lambda t: [
        t.transpose(PS2[jj][:, 0:16].bitcast(BF16)[0:32, 0:1],
                    nsc[0:1, 0:64].bitcast(BF16)[0:1, jj * 32:(jj + 1) * 32],
                    eye[0:1, 0:1]) for jj in range(2)], waits={"V": v11})
    v12 = p.stage("vector", lambda v: [
        v.tensor_copy(colb[0:32, jj:jj + 1], PS2[jj][:, 0:16].bitcast(BF16)[0:32, 0:1])
        for jj in range(2)], waits={"T": t4})
    t5 = p.stage("tensor", lambda t: [
        t.matmul(PS2[1][:, 16 + d * 2 + jj:17 + d * 2 + jj],
                 expm[0:32, d * 128:(d + 1) * 128], colb[0:32, jj:jj + 1],
                 start=True, stop=True) for d in range(NDT) for jj in range(2)],
        waits={"V": v12})
    v13 = p.stage("vector", lambda v: v.tensor_copy(colstat[:, 16:32], PS2[1][:, 16:32]),
                  waits={"T": t5})
    prev_v = v13
    for d in range(NDT):
        va = p.stage("vector", lambda v, d=d: v.tensor_scalar(
            nsc[:, :], dt_sl(ctx, d, NLOC), colstat[:, 16 + d * 2:16 + d * 2 + 1],
            colstat[:, 17 + d * 2:17 + d * 2 + 1], OP.subtract, OP.mult),
            waits={"V": prev_v})
        prev_v = p.stage("vector", lambda v, d=d: v.tensor_scalar(
            dt_sl(ctx, d, NLOC), nsc[:, :], gnwc[:, d:d + 1], gnwc[:, 8 + d:8 + d + 1],
            OP.mult, OP.add), waits={"V": va})

    wo0 = p.dma(lambda g, mark: load_wT(g, mark, wslot, A["l1a_outw"], NDT, 1024),
                waits={"T": p.val("T"), "W": p.val("W"), "S": p.val("S")}, n=1)
    obc = p.dma(lambda g, mark: mark(g.dma_start(
        out=colstat[:, 32:40], in_=A["l1a_outb"].rearrange("n p one -> p (n one)"),
        allow_slow_non_contiguous=True)), waits={"V": prev_v}, n=1)
    for qc in range(4):
        q0, q1 = qc * 512, (qc + 1) * 512
        for wv2 in range(2):
            t_o = p.stage("tensor", lambda t, q0=q0, q1=q1, wv2=wv2: [
                mm_acc(t, PS[i][:, :],
                       [wslot[:, k * 1024 + (wv2 * 4 + i) * 128:
                              k * 1024 + (wv2 * 4 + i + 1) * 128] for k in range(NDT)],
                       [dt_sl(ctx, k, NLOC, q0, q1) for k in range(NDT)])
                for i in range(4)], waits={"W": wo0, "V": prev_v})
            prev_v = p.stage("vector", lambda v, q0=q0, q1=q1, wv2=wv2: [
                (v.tensor_tensor(dt_sl(xT, wv2 * 4 + i, NLOC, q0, q1),
                                 dt_sl(xT, wv2 * 4 + i, NLOC, q0, q1), PS[i][:, :], OP.add),
                 v.tensor_scalar_add(dt_sl(xT, wv2 * 4 + i, NLOC, q0, q1),
                                     dt_sl(xT, wv2 * 4 + i, NLOC, q0, q1),
                                     colstat[:, 32 + wv2 * 4 + i:33 + wv2 * 4 + i]))[-1]
                for i in range(4)], waits={"T": t_o, "W": obc})
    ar.pop()
    ar.pop()
def emit_l1_ffn(p, st):
    A, sb, PS = st["A"], st["sb"], st["PS"]
    xT, xn, ar, consts = sb["xT"], sb["xn"], sb["arena"], sb["consts"]
    ar.push()
    a1 = ar.alloc(NDT * 24, BF16)
    r1 = ar.alloc(NDT * 3, BF16)
    b1m = ar.alloc(4096, BF16)
    b1c = ar.alloc(32, F32)
    a2 = ar.alloc(32 * 24, BF16)
    r2 = ar.alloc(32 * 3, BF16)
    b2m = ar.alloc(DM, BF16)
    b2c = ar.alloc(NDT, F32)
    zgbuf = ar.alloc(2 * NLOC, BF16)
    zg1 = zgbuf[0:24, 0:NLOC]
    zg2 = zgbuf[0:24, NLOC:2 * NLOC]
    rows = ar.alloc(2 * NLOC, BF16)
    rows2 = ar.alloc(2 * NLOC, BF16)
    erow = rows[0:3, 0:NLOC]
    e2row = rows[0:3, NLOC:2 * NLOC]
    rz = rows2[0:1, 0:NLOC]
    rz2 = rows2[0:1, NLOC:2 * NLOC]
    hb = ar.alloc(8 * 512, BF16)
    l2acc = ar.alloc(NLOC, F32)
    wh1 = ar.alloc(NDT * 1024, BF16)
    wh2 = ar.alloc(8 * DM, BF16)
    pre = p.snap()

    def fw0(g, mark):
        load_wT(g, mark, a1, A["l1f_a1"], NDT, 24)
        load_wT(g, mark, r1, A["l1f_r1"], NDT, 3)
        mark(g.dma_start(out=b1m[0:24, :], in_=A["l1f_b1m"]))
        mark(g.dma_start(out=b1c[:, :], in_=A["l1f_b1"].rearrange("n p one -> p (n one)"),
                         allow_slow_non_contiguous=True))
        load_wT(g, mark, a2, A["l1f_a2"], 32, 24)
        load_wT(g, mark, r2, A["l1f_r2"], 32, 3)
        mark(g.dma_start(out=b2m[0:24, :], in_=A["l1f_b2m"]))
        mark(g.dma_start(out=b2c[:, :], in_=A["l1f_b2"].rearrange("n p one -> p (n one)"),
                         allow_slow_non_contiguous=True))
    w0 = p.dma(fw0, waits=pre, n=8)

    ones = consts[:, 0:128]
    pat24 = consts[:, 1408:1432]

    def lreg(ps_lo, ps_hi, c, n):
        bank = ps_lo if c < 2 else ps_hi
        b0 = (c % 2) * 32
        return bank[b0:b0 + n, :]

    def tg1(t):
        last = None
        for c in range(4):
            last = mm_acc(t, lreg(PS[0], PS[7], c, 3),
                          [r1[:, k * 3:(k + 1) * 3] for k in range(NDT)],
                          [dt_sl(xn, k, NLOC, c * 512, (c + 1) * 512) for k in range(NDT)])
        for c in range(4):
            last = mm_acc(t, PS[1 + c][0:24, :],
                          [a1[:, k * 24:(k + 1) * 24] for k in range(NDT)],
                          [dt_sl(xn, k, NLOC, c * 512, (c + 1) * 512) for k in range(NDT)])
        return last
    t1 = p.stage("tensor", tg1, waits={"W": w0, "V": p.val("V")})
    s1 = p.stage("scalar", lambda s: [
        s.activation(erow[:, c * 512:(c + 1) * 512], lreg(PS[0], PS[7], c, 3), ACT.Exp)
        for c in range(4)], waits={"T": t1})
    t2 = p.stage("tensor", lambda t: [
        t.matmul(lreg(PS[5], PS[6], c, 1), ones[0:3, 0:1],
                 erow[:, c * 512:(c + 1) * 512], start=True, stop=True)
        for c in range(4)], waits={"S": s1})
    def fv1(v):
        with p.nc.allow_low_precision(reason="bf16 gate reciprocal (tolerance-ok)"):
            last = None
            for c in range(4):
                last = v.reciprocal(rz[:, c * 512:(c + 1) * 512], lreg(PS[5], PS[6], c, 1))
        return last
    v1 = p.stage("vector", fv1, waits={"T": t2})
    v1b = p.stage("vector", lambda v:
                  v.tensor_scalar_mul(rz[:, :], rz[:, :], 0.125), waits={"V": v1})
    prev_done = v1b
    for c in range(4):
        cs = slice(c * 512, (c + 1) * 512)
        t3c = p.stage("tensor", lambda t: [
            t.matmul(PS[5][0:24, :], pat24[0:3, 0:24], erow[:, cs], start=True, stop=True),
            t.matmul(PS[6][0:24, :], ones[0:1, 0:24], rz[:, cs], start=True, stop=True)][-1],
            waits={"V": prev_done})
        vz0 = p.stage("vector", lambda v:
                      v.tensor_copy(zg1[:, cs], PS[1 + c][0:24, :]), waits={"T": t3c})
        vz = p.stage("vector", lambda v:
                     v.tensor_tensor(zg1[:, cs], zg1[:, cs], PS[5][0:24, :], OP.mult),
                     waits={"V": vz0})
        prev_done = p.stage("vector", lambda v:
                            v.tensor_tensor(zg1[:, cs], zg1[:, cs], PS[6][0:24, :], OP.mult),
                            waits={"V": vz})

    prev_t = p.val("T")
    prev_v = prev_done
    zl_marks = []
    NHH = 4
    for hh in range(NHH):
        def fwH(g, mark):
            load_wT(g, mark, wh1, A["l1f_w1"][:, hh * 1024:(hh + 1) * 1024], NDT, 1024)
            load_wT(g, mark, wh2, A["l1f_w2"][hh * 1024:(hh + 1) * 1024, :], 8, DM)
        wH = p.dma(fwH, waits={"T": prev_t, "V": prev_v, "W": p.val("W")}, n=2)
        for c in range(4):
            cs0, cs1 = c * 512, (c + 1) * 512
            rhs = [dt_sl(xn, k, NLOC, cs0, cs1) for k in range(NDT)]
            smarks = []
            for hp in range(8):
                bk = hp % 2
                wv = {"W": wH, "V": prev_v}
                if len(smarks) >= 2:
                    wv["S"] = smarks[-2]
                hpg = hh * 8 + hp
                tm = p.stage("tensor", lambda t: [
                    mm_acc(t, PS[bk][:, :],
                           [wh1[:, k * 1024 + hp * 128: k * 1024 + (hp + 1) * 128]
                            for k in range(NDT)], rhs, stop=False),
                    t.matmul(PS[bk][:, :], b1m[0:24, hpg * 128:(hpg + 1) * 128],
                             zg1[:, cs0:cs1], start=False, stop=True)][-1],
                    waits=wv)
                smarks.append(p.stage("scalar", lambda s: s.activation(
                    hb[:, hp * 512:(hp + 1) * 512], PS[bk][:, :], ACT.Relu,
                    bias=b1c[:, hpg:hpg + 1]), waits={"T": tm}))
            for wv2 in range(4):
                t_o = p.stage("tensor", lambda t: [
                    mm_acc(t, PS[i][:, :],
                           [wh2[:, k * DM + (wv2 * 2 + i) * 128: k * DM + (wv2 * 2 + i + 1) * 128]
                            for k in range(8)],
                           [hb[:, k * 512:(k + 1) * 512] for k in range(8)])
                    for i in range(2)], waits={"S": smarks[-1], "V": prev_v})
                prev_v = p.stage("vector", lambda v: [
                    v.tensor_tensor(dt_sl(xT, wv2 * 2 + i, NLOC, cs0, cs1),
                                    dt_sl(xT, wv2 * 2 + i, NLOC, cs0, cs1),
                                    PS[i][:, :], OP.add) for i in range(2)],
                    waits={"T": t_o})
                prev_t = t_o
            tzl = p.stage("tensor", lambda t: [
                mm_acc(t, PS[4 + c][0:24, :],
                       [a2[:, (hh * 8 + k) * 24:(hh * 8 + k + 1) * 24] for k in range(8)],
                       [hb[:, k * 512:(k + 1) * 512] for k in range(8)],
                       start=(hh == 0), stop=(hh == NHH - 1)),
                mm_acc(t, PS[2][0:3, :],
                       [r2[:, (hh * 8 + k) * 3:(hh * 8 + k + 1) * 3] for k in range(8)],
                       [hb[:, k * 512:(k + 1) * 512] for k in range(8)])][-1],
                waits={"S": smarks[-1], "V": prev_v})
            if hh == 0:
                vl2 = p.stage("vector", lambda v: v.tensor_copy(
                    l2acc[0:3, cs0:cs1], PS[2][0:3, :]), waits={"T": tzl})
            else:
                vl2 = p.stage("vector", lambda v: v.tensor_tensor(
                    l2acc[0:3, cs0:cs1], l2acc[0:3, cs0:cs1], PS[2][0:3, :], OP.add),
                    waits={"T": tzl})
            prev_t = tzl
            prev_v = vl2
            if hh == NHH - 1:
                zl_marks.append(vl2)

    s2 = p.stage("scalar", lambda s: [
        s.activation(e2row[:, c * 512:(c + 1) * 512], l2acc[0:3, c * 512:(c + 1) * 512],
                     ACT.Exp) for c in range(4)], waits={"V": zl_marks[-1]})
    t4 = p.stage("tensor", lambda t: [
        t.matmul(lreg(PS[0], PS[1], c, 1), ones[0:3, 0:1],
                 e2row[:, c * 512:(c + 1) * 512], start=True, stop=True)
        for c in range(4)], waits={"S": s2, "V": prev_v})
    def fv2a(v):
        with p.nc.allow_low_precision(reason="bf16 gate reciprocal (tolerance-ok)"):
            last = None
            for c in range(4):
                last = v.reciprocal(rz2[:, c * 512:(c + 1) * 512], lreg(PS[0], PS[1], c, 1))
        return last
    v2a = p.stage("vector", fv2a, waits={"T": t4})
    v2b = p.stage("vector", lambda v:
                  v.tensor_scalar_mul(rz2[:, :], rz2[:, :], 0.125), waits={"V": v2a})
    prev_done = v2b
    for c in range(4):
        cs = slice(c * 512, (c + 1) * 512)
        t5 = p.stage("tensor", lambda t: [
            t.matmul(PS[0][0:24, :], pat24[0:3, 0:24], e2row[:, cs], start=True, stop=True),
            t.matmul(PS[1][0:24, :], ones[0:1, 0:24], rz2[:, cs], start=True, stop=True)][-1],
            waits={"V": prev_done})
        vz0 = p.stage("vector", lambda v:
                      v.tensor_copy(zg2[:, cs], PS[4 + c][0:24, :]), waits={"T": t5})
        vz = p.stage("vector", lambda v:
                     v.tensor_tensor(zg2[:, cs], zg2[:, cs], PS[0][0:24, :], OP.mult),
                     waits={"V": vz0})
        prev_done = p.stage("vector", lambda v:
                            v.tensor_tensor(zg2[:, cs], zg2[:, cs], PS[1][0:24, :], OP.mult),
                            waits={"V": vz})
    for c in range(4):
        cs0, cs1 = c * 512, (c + 1) * 512
        t6 = p.stage("tensor", lambda t: [
            t.matmul(PS[i][:, :], b2m[0:24, i * 128:(i + 1) * 128], zg2[:, cs0:cs1],
                     start=True, stop=True) for i in range(NDT)],
            waits={"V": prev_done if c == 0 else prev_v, "T": p.val("T")})
        prev_v = p.stage("vector", lambda v: [
            (v.tensor_tensor(dt_sl(xT, i, NLOC, cs0, cs1), dt_sl(xT, i, NLOC, cs0, cs1),
                             PS[i][:, :], OP.add),
             v.tensor_scalar_add(dt_sl(xT, i, NLOC, cs0, cs1), dt_sl(xT, i, NLOC, cs0, cs1),
                                 b2c[:, i:i + 1]))[-1] for i in range(NDT)],
            waits={"T": t6})
    ar.pop()
def emit_l1_attn_b(p, st):
    A, sb, PS = st["A"], st["sb"], st["PS"]
    PS2 = st["PS2"]
    xT, xn, ar, constf = sb["xT"], sb["xn"], sb["arena"], sb["constf"]
    NDI = 16
    pre = p.snap()
    ar.push()
    rsel = ar.alloc(32 * 128, BF16)
    wsl = ar.alloc(16 * 128, BF16)
    wdtsb = ar.alloc(2048, BF16)
    wxsb = ar.alloc(16 * 96, BF16)
    cols = ar.alloc(64, F32)
    acol = ar.alloc(256, F32)
    wcv = ar.alloc(64, F32)
    carry = ar.alloc(256, F32)
    stash = ar.alloc(48, BF16)
    edg = ar.alloc(48, F32)
    dtT = ar.alloc(512, BF16)
    bcsb = ar.alloc(512, BF16)
    big = ar.alloc(32 * 512, BF16)
    dasl = ar.alloc(512, F32)
    dbsl = ar.alloc(512, BF16)
    htile = ar.alloc(512, F32)
    ytile = ar.alloc(512, F32)
    delt1 = ar.alloc(512, F32)
    dut = ar.alloc(512, BF16)
    ztile = ar.alloc(512, BF16)
    ystg = ar.alloc(16 * 512, BF16)
    uch = ar.alloc(16 * 512, BF16)
    dstg = big[:, 0:8 * 256]

    w0 = p.dma(lambda g, mark: mark(g.dma_start(out=rsel[0:32, :], in_=A["rowsel32"])),
               waits=pre, n=1)

    for dirn in ("f", "b"):
        rev = dirn == "b"
        ccs = [3, 2, 1, 0] if rev else [0, 1, 2, 3]
        mask = constf[:, 1:2] if rev else constf[:, 0:1]
        slab = 1 if rev else 0

        def fcols(g, mark, dirn=dirn):
            mark(g.dma_start(out=cols[:, 0:16], in_=A[f"mb_bconv_{dirn}"]))
            mark(g.dma_start(out=cols[:, 16:32], in_=A[f"mb_bdt_{dirn}"]))
            mark(g.dma_start(out=cols[:, 32:48], in_=A[f"mb_dp_{dirn}"]))
            mark(g.dma_start(out=wcv[:, :], in_=A[f"mb_wconv_{dirn}"]))
            mark(g.dma_start(out=acol[:, :], in_=A[f"mb_alog_{dirn}"]))
            mark(g.dma_start(out=wxsb[:, :].rearrange("p (n m) -> p n m", n=16),
                             in_=A[f"mb_wx_{dirn}"].rearrange("(n p) m -> p n m", p=128)))
            mark(g.dma_start(out=wdtsb[0:64, :], in_=A[f"mb_wdt_{dirn}"]))
        wc = p.dma(fcols, waits={"W": w0, "T": p.val("T"), "V": p.val("V"),
                                 "S": p.val("S")}, n=7)
        sA = p.stage("scalar", lambda s: s.activation(acol[:, :], acol[:, :], ACT.Exp),
                     waits={"W": wc})
        vA = p.stage("vector", lambda v: v.tensor_single_scalar(
            acol[:, :], acol[:, :], -1.0, OP.mult), waits={"S": sA})

        e0 = 0 if rev else NLOC - 3
        sev = p.val("S")
        for dd in range(NDI):
            wme = p.dma(lambda g, mark, dd=dd: load_wT(
                g, mark, wsl, A[f"mb_win_{dirn}"][:, dd * 128:(dd + 1) * 128], NDT, 128),
                waits={"T": p.val("T")}, n=1)
            te = p.stage("tensor", lambda t, dd=dd: mm_acc(
                t, PS[2][:, 0:3],
                [wsl[:, k * 128:(k + 1) * 128] for k in range(NDT)],
                [dt_sl(xn, k, NLOC, e0, e0 + 3) for k in range(NDT)]),
                waits={"W": wme, "S": sev, "V": vA})
            sev = p.stage("scalar", lambda s, dd=dd: s.activation(
                edg[:, dd * 3:(dd + 1) * 3], PS[2][:, 0:3], ACT.Identity),
                waits={"T": te})
        gx = exchange_small(p, st, edg[:, 0:48], 128, 48, {"S": sev})
        me = p.dma(lambda g, mark: mark(g.dma_start(
            out=edg[:, :],
            in_=st["cc"]["out_small"].ap()[slab * 128:slab * 128 + 128, 0:48])),
            waits={"G": gx}, n=1, sem="X")
        vme = p.stage("vector", lambda v: v.tensor_single_scalar(
            edg[:, :], edg[:, :], mask, OP.mult), waits={"X": me})

        for pss in range(2):
            if pss == 0:
                vcm = p.stage("vector", lambda v: v.memset(carry[:, :], 0.0),
                              waits={"V": vme})
            else:
                gx2 = exchange_small(p, st, carry[:, 0:256], 128, 256,
                                     {"V": p.val("V")})
                me2 = p.dma(lambda g, mark: mark(g.dma_start(
                    out=carry[:, :],
                    in_=st["cc"]["out_small"].ap()[slab * 128:slab * 128 + 128, 0:256])),
                    waits={"G": gx2}, n=1, sem="X")
                vcm = p.stage("vector", lambda v: v.tensor_single_scalar(
                    carry[:, :], carry[:, :], mask, OP.mult), waits={"X": me2})

            for cc in ccs:
                c0 = cc * 512
                first = cc == ccs[0]
                scv = p.val("S")
                for dd in range(NDI):
                    off = dd * 515
                    wmu = p.dma(lambda g, mark, dd=dd: load_wT(
                        g, mark, wsl, A[f"mb_win_{dirn}"][:, dd * 128:(dd + 1) * 128],
                        NDT, 128), waits={"T": p.val("T")}, n=1)
                    tu = p.stage("tensor", lambda t, cc=cc: mm_acc(
                        t, PS[2][:, :],
                        [wsl[:, k * 128:(k + 1) * 128] for k in range(NDT)],
                        [dt_sl(xn, k, NLOC, c0, c0 + 512) for k in range(NDT)]),
                        waits={"W": wmu, "S": p.val("S"), "V": p.val("V")})
                    dcol = off + (0 if rev else 3)
                    su = p.stage("scalar", lambda s, dcol=dcol: s.activation(
                        big[:, dcol:dcol + 512], PS[2][:, :], ACT.Identity),
                        waits={"T": tu})
                    pcol = off + (512 if rev else 0)
                    tail = off + (3 if rev else 512)
                    def fpad(v, dd=dd, pcol=pcol, tail=tail, first=first):
                        esrc = edg[:, dd * 3:(dd + 1) * 3] if first else \
                            stash[:, dd * 3:(dd + 1) * 3]
                        v.tensor_copy(big[:, pcol:pcol + 3], esrc)
                        return v.tensor_copy(stash[:, dd * 3:(dd + 1) * 3],
                                             big[:, tail:tail + 3])
                    vp = p.stage("vector", fpad, waits={"S": su})
                    def fconv(v, dd=dd, off=off):
                        last = None
                        for j in range(4):
                            srcap = big[:, off + (3 - j if rev else j):
                                        off + (3 - j if rev else j) + 512]
                            wj = wcv[:, dd * 4 + j:dd * 4 + j + 1]
                            if j == 0:
                                last = v.tensor_single_scalar(htile[:, :], srcap,
                                                              wj, OP.mult)
                            else:
                                last = v.scalar_tensor_tensor(htile[:, :], srcap, wj,
                                                              htile[:, :], OP.mult,
                                                              OP.add)
                        return last
                    vcv = p.stage("vector", fconv, waits={"V": vp, "S": scv})
                    scv = p.stage("scalar", lambda s, dd=dd: s.activation(
                        uch[:, dd * 512:(dd + 1) * 512], htile[:, :], ACT.Silu,
                        bias=cols[:, dd:dd + 1]), waits={"V": vcv})

                tdb = p.stage("tensor", lambda t: [
                    mm_acc(t, PS[0][0:64, :],
                           [wxsb[:, k * 96:k * 96 + 64] for k in range(16)],
                           [uch[:, k * 512:(k + 1) * 512] for k in range(16)]),
                    mm_acc(t, PS[1][0:32, :],
                           [wxsb[:, k * 96 + 64:k * 96 + 96] for k in range(16)],
                           [uch[:, k * 512:(k + 1) * 512] for k in range(16)])][-1],
                    waits={"S": scv, "V": p.val("V")})
                sdb = p.stage("scalar", lambda s: [
                    s.activation(dtT[0:64, :], PS[0][0:64, :], ACT.Identity),
                    s.activation(bcsb[0:32, :], PS[1][0:32, :], ACT.Identity)][-1],
                    waits={"T": tdb})

                vbm = p.val("V")
                for n in range(16):
                    def fb(t, n=n, pss=pss):
                        last = t.matmul(PS2[0][:, :], rsel[0:32, n * 128:(n + 1) * 128],
                                        bcsb[0:32, :], start=True, stop=True)
                        if pss == 1:
                            last = t.matmul(PS2[1][:, :],
                                            rsel[0:32, (16 + n) * 128:(17 + n) * 128],
                                            bcsb[0:32, :], start=True, stop=True)
                        return last
                    tb = p.stage("tensor", fb, waits={"S": sdb, "V": vbm})
                    def fbc(v, n=n, pss=pss):
                        last = v.tensor_copy(big[:, n * 512:(n + 1) * 512], PS2[0][:, :])
                        if pss == 1:
                            last = v.tensor_copy(big[:, (16 + n) * 512:(17 + n) * 512],
                                                 PS2[1][:, :])
                        return last
                    vbm = p.stage("vector", fbc, waits={"T": tb})

                szp = p.val("S")
                for dd in range(NDI):
                    bank = 2 + dd % 2
                    tdl = p.stage("tensor", lambda t, dd=dd, bank=bank: t.matmul(
                        PS[bank][:, :], wdtsb[0:64, dd * 128:(dd + 1) * 128],
                        dtT[0:64, :], start=True, stop=True),
                        waits={"S": p.val("S"), "V": p.val("V")})
                    sd1 = p.stage("scalar", lambda s, dd=dd, bank=bank: s.activation(
                        htile[:, :], PS[bank][:, :], ACT.Exp,
                        bias=cols[:, 16 + dd:17 + dd]),
                        waits={"T": tdl, "V": p.val("V")})
                    sdl = p.stage("scalar", lambda s: s.activation(
                        delt1[:, :], htile[:, :], ACT.Ln, bias=constf[:, 2:3]),
                        waits={"S": sd1})
                    vdu = p.stage("vector", lambda v, dd=dd: v.tensor_tensor(
                        dut[:, :], delt1[:, :], uch[:, dd * 512:(dd + 1) * 512],
                        OP.mult), waits={"S": sdl})
                    if pss == 1:
                        vg = p.stage("vector", lambda v: v.memset(ytile[:, :], 0.0),
                                     waits={"V": vdu})
                    else:
                        vg = vdu
                    dvw = delt1[:, ::-1] if rev else delt1[:, :]
                    duw = dut[:, ::-1] if rev else dut[:, :]
                    for n in range(16):
                        sda = p.stage("scalar", lambda s, dd=dd, n=n: s.activation(
                            dasl[:, :], dvw, ACT.Exp,
                            scale=acol[:, dd * 16 + n:dd * 16 + n + 1]),
                            waits={"V": vg})
                        bw = big[:, n * 512:(n + 1) * 512]
                        if rev:
                            bw = bw[:, ::-1]
                        vdb = p.stage("vector", lambda v, bw=bw: v.tensor_tensor(
                            dbsl[:, :], duw, bw, OP.mult), waits={"V": vg})
                        vsc = p.stage("vector", lambda v, dd=dd, n=n: v.tensor_tensor_scan(
                            htile[:, :], dasl[:, :], dbsl[:, :],
                            carry[:, dd * 16 + n:dd * 16 + n + 1], OP.mult, OP.add),
                            waits={"S": sda, "V": vdb})
                        vca = p.stage("vector", lambda v, dd=dd, n=n: v.tensor_copy(
                            carry[:, dd * 16 + n:dd * 16 + n + 1], htile[:, 511:512]),
                            waits={"V": vsc})
                        if pss == 1:
                            cw = big[:, (16 + n) * 512:(17 + n) * 512]
                            if rev:
                                cw = cw[:, ::-1]
                            vy1 = p.stage("vector", lambda v, cw=cw: v.tensor_tensor(
                                htile[:, :], htile[:, :], cw, OP.mult),
                                waits={"V": vca})
                            vg = p.stage("vector", lambda v: v.tensor_tensor(
                                ytile[:, :], ytile[:, :], htile[:, :], OP.add),
                                waits={"V": vy1})
                        else:
                            vg = vca
                    if pss == 1:
                        ucw = uch[:, dd * 512:(dd + 1) * 512]
                        if rev:
                            ucw = ucw[:, ::-1]
                        vdp = p.stage("vector", lambda v, dd=dd, ucw=ucw: v.scalar_tensor_tensor(
                            ytile[:, :], ucw, cols[:, 32 + dd:33 + dd], ytile[:, :],
                            OP.mult, OP.add), waits={"V": vg})
                        wmz = p.dma(lambda g, mark, dd=dd: load_wT(
                            g, mark, wsl,
                            A[f"mb_win_{dirn}"][:, 2048 + dd * 128:2048 + (dd + 1) * 128],
                            NDT, 128), waits={"T": p.val("T")}, n=1)
                        tz = p.stage("tensor", lambda t, cc=cc: mm_acc(
                            t, PS[4][:, :],
                            [wsl[:, k * 128:(k + 1) * 128] for k in range(NDT)],
                            [dt_sl(xn, k, NLOC, c0, c0 + 512) for k in range(NDT)]),
                            waits={"W": wmz, "S": szp})
                        szp = p.stage("scalar", lambda s: s.activation(
                            ztile[:, :], PS[4][:, :], ACT.Silu), waits={"T": tz})
                        def fsty(v, dd=dd, rev=rev):
                            last = None
                            for h in range(2):
                                if rev:
                                    ysrc = ytile[:, (1 - h) * 256:(2 - h) * 256][:, ::-1]
                                else:
                                    ysrc = ytile[:, h * 256:(h + 1) * 256]
                                last = v.tensor_tensor(
                                    ystg[:, dd * 512 + h * 256:dd * 512 + h * 256 + 256],
                                    ysrc, ztile[:, h * 256:(h + 1) * 256], OP.mult)
                            return last
                        vg = p.stage("vector", fsty,
                                     waits={"S": szp, "V": vdp, "T": p.val("T")})

                if pss == 0:
                    continue
                for h in range(2):
                    so = p.val("S")
                    for o in range(8):
                        wmo = p.dma(lambda g, mark, o=o: load_wT(
                            g, mark, wsl, A[f"mb_wout_{dirn}"][:, o * 128:(o + 1) * 128],
                            16, 128), waits={"T": p.val("T")}, n=1)
                        t_o = p.stage("tensor", lambda t, h=h: mm_acc(
                            t, PS[5][:, 0:256],
                            [wsl[:, k * 128:(k + 1) * 128] for k in range(16)],
                            [ystg[:, k * 512 + h * 256:k * 512 + h * 256 + 256]
                             for k in range(16)]),
                            waits={"W": wmo, "S": so, "V": p.val("V")})
                        so = p.stage("scalar", lambda s, o=o: s.activation(
                            dstg[:, o * 256:(o + 1) * 256], PS[5][:, 0:256],
                            ACT.Identity), waits={"T": t_o})
                    r0 = 1024 if rev else 0
                    vxm = p.val("V")
                    for o in range(8):
                        wmp = p.dma(lambda g, mark, o=o, r0=r0: load_wT(
                            g, mark, wsl,
                            A["mb_wproj"][r0:r0 + 1024, o * 128:(o + 1) * 128],
                            8, 128), waits={"T": p.val("T")}, n=1)
                        t_p = p.stage("tensor", lambda t: mm_acc(
                            t, PS[0][:, 0:256],
                            [wsl[:, k * 128:(k + 1) * 128] for k in range(8)],
                            [dstg[:, k * 256:(k + 1) * 256] for k in range(8)]),
                            waits={"W": wmp, "S": so, "V": vxm})
                        vxm = p.stage("vector", lambda v, o=o, h=h, c0=c0: v.tensor_tensor(
                            dt_sl(xT, o, NLOC, c0 + h * 256, c0 + h * 256 + 256),
                            dt_sl(xT, o, NLOC, c0 + h * 256, c0 + h * 256 + 256),
                            PS[0][:, 0:256], OP.add), waits={"T": t_p})

    wb = p.dma(lambda g, mark: mark(g.dma_start(
        out=cols[:, 48:56], in_=A["mb_bproj"])), waits={"V": p.val("V")}, n=1)
    p.stage("vector", lambda v: [v.tensor_single_scalar(
        dt_sl(xT, i, NLOC), dt_sl(xT, i, NLOC), cols[:, 48 + i:49 + i], OP.add)
        for i in range(NDT)][-1], waits={"W": wb, "V": p.val("V")})
    ar.pop()
def emit_l2_attn_a(p, st):
    A, sb, PS = st["A"], st["sb"], st["PS"]
    xT, xn, ar, consts, constf = sb["xT"], sb["xn"], sb["arena"], sb["consts"], sb["constf"]
    ones = consts[:, 0:128]
    onesrowf = constf[0:1, 160:288]
    PS2 = st["PS2"]
    cc = st["cc"]
    cc_done = exchange_xn(p, st)

    ar.push()
    onb = ar.alloc(NDT * NLOC, BF16)
    lamc = ar.alloc(2, F32)
    nwb = ar.alloc(16, F32)
    ar.push()
    xfk = ar.alloc(SFULL, BF16)
    wsl = ar.alloc(3 * NDT * 64, BF16)
    kt = ar.alloc(SFULL, BF16)
    vnat = ar.alloc(32 * 65, BF16)
    qt = ar.alloc(NLOC, BF16)
    qsq = ar.alloc(NLOC, BF16)
    ksqc = ar.alloc(512, BF16)
    U = ar.alloc(2 * 512, F32)
    sc = ar.alloc(4 * 512, F32)
    pt = ar.alloc(4 * 512, BF16)
    osb = ar.alloc(2 * 512, BF16)
    kmx = ar.alloc(2, F32)
    nr = ar.alloc(2 * 512, F32)
    krow = ar.alloc(24, F32)
    lrow = ar.alloc(8, F32)
    rmsb = ar.alloc(512, BF16)
    rmsf = ar.alloc(512, F32)

    lm0 = p.dma(lambda g, mark: [
        mark(g.dma_start(out=lrow[0:64, q:q + 1], in_=A[f"l2a_lambda_{nm}"]))
        for q, nm in enumerate(["q1", "k1", "q2", "k2"])],
        waits=p.snap("V", "S", "T", "G"), n=4)
    lv1 = p.stage("vector", lambda v: [
        v.tensor_tensor(lrow[0:64, 4:5], lrow[0:64, 0:1], lrow[0:64, 1:2], OP.mult),
        v.tensor_tensor(lrow[0:64, 5:6], lrow[0:64, 2:3], lrow[0:64, 3:4], OP.mult)][-1],
        waits={"W": lm0})
    lt1 = p.stage("tensor", lambda t: [
        t.matmul(PS2[j][0:1, 0:1], constf[0:64, 2:3], lrow[0:64, 4 + j:5 + j],
                 start=True, stop=True) for j in range(2)], waits={"V": lv1})
    ls1 = p.stage("scalar", lambda s: [
        s.activation(lrow[0:1, 6 + j:7 + j], PS2[j][0:1, 0:1], ACT.Exp)
        for j in range(2)], waits={"T": lt1})
    lv2 = p.stage("vector", lambda v: v.tensor_tensor(
        lrow[0:1, 6:7], lrow[0:1, 6:7], lrow[0:1, 7:8], OP.subtract), waits={"S": ls1})
    lv3 = p.stage("vector", lambda v: v.tensor_scalar_add(lrow[0:1, 6:7], lrow[0:1, 6:7],
                                                          0.8), waits={"V": lv2})
    lv4 = p.stage("vector", lambda v: v.tensor_scalar_mul(lrow[0:1, 7:8], lrow[0:1, 6:7],
                                                          -1.0), waits={"V": lv3})
    lt2 = p.stage("tensor", lambda t: t.matmul(
        PS2[0][:, 0:1], onesrowf, lrow[0:1, 7:8], start=True, stop=True),
        waits={"V": lv4})
    lv5 = p.stage("vector", lambda v: v.tensor_copy(lamc[:, 0:1], PS2[0][:, 0:1]),
                  waits={"T": lt2})

    nw0 = p.dma(lambda g, mark: [
        mark(g.dma_start(out=nwb[(grp % 2) * 64:(grp % 2) * 64 + 64, grp:grp + 1],
                         in_=A["l2a_normw"][grp])) for grp in range(16)],
        waits=p.snap("V"), n=16)

    prev_v = lv5
    prev_s = p.val("S")
    for grp in range(16):
        r0 = (grp % 2) * 64
        def fwg(g, mark, grp=grp):
            for j, base in enumerate([0, 1024, 2048]):
                mark(g.dma_start(
                    out=wsl[:, j * NDT * 64:(j + 1) * NDT * 64].rearrange(
                        "p (n m) -> p n m", n=NDT),
                    in_=A["l2a_wqkv"][:, base + grp * 64:base + (grp + 1) * 64].rearrange(
                        "(n p) m -> p n m", p=128)))
        wgm = p.dma(fwg, waits={"T": p.val("T"), "V": prev_v, "S": prev_s}, n=3)

        tk = 0
        for k in range(NDT):
            xm = p.dma(lambda g, mark, k=k: load_xfk(g, mark, st, xfk, k),
                       waits={"G": cc_done, "T": tk, "V": prev_v, "S": prev_s},
                       n=2, sem="X")
            tk = p.stage("tensor", lambda t, k=k: [
                t.matmul(PS[ci][0:64, :], wsl[:, NDT * 64 + k * 64:NDT * 64 + (k + 1) * 64],
                         xfk[:, ci * 512:(ci + 1) * 512],
                         start=(k == 0), stop=(k == NDT - 1)) for ci in range(8)],
                waits={"X": xm, "W": wgm})
        sk = p.stage("scalar", lambda s: [
            s.activation(kt[0:64, ci * 512:(ci + 1) * 512], PS[ci][0:64, :], ACT.Identity)
            for ci in range(8)], waits={"T": tk})
        vmx = 0
        for hf in range(2):
            for ci in range(8):
                ssq = p.stage("scalar", lambda s, hf=hf, ci=ci: s.activation(
                    ksqc[0:32, :], kt[hf * 32:hf * 32 + 32, ci * 512:(ci + 1) * 512],
                    ACT.Square), waits={"S": sk, "V": vmx})
                tr = p.stage("tensor", lambda t, ci=ci, hf=hf: t.matmul(
                    PS2[ci % 2][0:1, :], ones[hf * 32:hf * 32 + 32, 0:1]
                    if False else ones[0:32, 0:1],
                    ksqc[0:32, :], start=True, stop=True), waits={"S": ssq})
                vmx = p.stage("vector", lambda v, hf=hf, ci=ci: v.reduce_max(
                    krow[0:1, hf * 8 + ci:hf * 8 + ci + 1], PS2[ci % 2][0:1, :],
                    axis=AX), waits={"T": tr})
        vmx2 = p.stage("vector", lambda v: [
            v.reduce_max(krow[0:1, 16 + hf:17 + hf], krow[0:1, hf * 8:(hf + 1) * 8],
                         axis=AX) for hf in range(2)], waits={"V": vmx})
        tkb = p.stage("tensor", lambda t: [
            t.matmul(PS2[hf][:, 0:1], onesrowf, krow[0:1, 16 + hf:17 + hf],
                     start=True, stop=True) for hf in range(2)], waits={"V": vmx2})
        vkm = p.stage("vector", lambda v: [
            v.tensor_copy(kmx[:, hf:hf + 1], PS2[hf][:, 0:1]) for hf in range(2)],
            waits={"T": tkb})

        vm = 0
        for w in range(4):
            tv = 0
            for k in range(NDT):
                def fxq(g, mark, k=k, w=w):
                    ob = cc["out_big"].ap()
                    r = w // 2
                    mark(g.dma_start(
                        out=xfk[:, 0:1024] if k % 2 == 0 else xfk[:, 1024:2048],
                        in_=ob[r * DM + k * 128: r * DM + (k + 1) * 128,
                               (w % 2) * 1024:(w % 2 + 1) * 1024]))
                xm = p.dma(fxq, waits={"G": cc_done, "T": tv,
                                       "V": vkm if (k == 0 and w == 0) else vm,
                                       "S": sk if (k == 0 and w == 0) else 0},
                           n=1, sem="X")
                tv = p.stage("tensor", lambda t, k=k: [
                    t.matmul(PS[b][:, 0:64],
                             xfk[:, (k % 2) * 1024 + b * 128:(k % 2) * 1024 + (b + 1) * 128],
                             wsl[:, 2 * NDT * 64 + k * 64:2 * NDT * 64 + (k + 1) * 64],
                             start=(k == 0), stop=(k == NDT - 1)) for b in range(8)],
                    waits={"X": xm})
            vm = p.stage("vector", lambda v, w=w: [
                v.tensor_copy(vnat[:, (w * 8 + b) * 65:(w * 8 + b) * 65 + 64],
                              PS[b][:, 0:64]) for b in range(8)], waits={"T": tv})
        vm = p.stage("vector", lambda v: v.memset(
            bass.AP(vnat.tensor, vnat.offset + 64, [[vnat.ap[0][0], 128], [65, 32]]),
            1.0), waits={"V": vm})

        tq = p.stage("tensor", lambda t: [
            mm_acc(t, PS[c][0:64, :],
                   [wsl[:, k * 64:(k + 1) * 64] for k in range(NDT)],
                   [dt_sl(xn, k, NLOC, c * 512, (c + 1) * 512) for k in range(NDT)])
            for c in range(4)], waits={"V": vm})
        sq1 = p.stage("scalar", lambda s: [
            s.activation(qt[0:64, c * 512:(c + 1) * 512], PS[c][0:64, :], ACT.Identity)
            for c in range(4)], waits={"T": tq})
        sq2 = p.stage("scalar", lambda s: [
            s.activation(qsq[0:64, c * 512:(c + 1) * 512], PS[c][0:64, :], ACT.Square)
            for c in range(4)], waits={"T": tq})

        exp_marks = {}
        out_marks = prev_v
        for qc in range(4):
            q0, q1 = qc * 512, (qc + 1) * 512
            tub = p.stage("tensor", lambda t: [
                t.matmul(PS2[hf][:, :], ones[hf * 32:hf * 32 + 32, 0:128],
                         qsq[hf * 32:hf * 32 + 32, q0:q1], start=True, stop=True)
                for hf in range(2)], waits={"S": sq2, "V": out_marks})
            su = p.stage("scalar", lambda s: [
                s.activation(U[:, hf * 512:(hf + 1) * 512], PS2[hf][:, :], ACT.Sqrt,
                             scale=kmx[:, hf:hf + 1]) for hf in range(2)],
                waits={"T": tub, "V": vkm})
            for stt in range(33):
                def ft(t, stt=stt):
                    last = None
                    if stt > 0:
                        sp = stt - 1
                        for hf in range(2):
                            ptile = pt[:, (2 * hf + sp % 2) * 512:(2 * hf + sp % 2 + 1) * 512]
                            last = t.matmul(PS[4 + hf][r0:r0 + 64, :],
                                            vnat[:, sp * 65:sp * 65 + 64], ptile,
                                            start=(sp == 0), stop=(sp == 31))
                            last = t.matmul(PS2[hf][0:1, :],
                                            vnat[:, sp * 65 + 64:sp * 65 + 65], ptile,
                                            start=(sp == 0), stop=(sp == 31))
                    if stt < 32:
                        for hf in range(2):
                            last = t.matmul(PS[2 * hf + stt % 2][:, :],
                                            kt[hf * 32:hf * 32 + 32, stt * 128:(stt + 1) * 128],
                                            qt[hf * 32:hf * 32 + 32, q0:q1],
                                            start=True, stop=True)
                    return last
                wv_t = {"S": exp_marks.get(stt - 1, 0)}
                if stt == 0:
                    wv_t["S"] = max(wv_t["S"], su)
                if stt >= 2:
                    wv_t["S"] = max(wv_t["S"], exp_marks.get(stt - 2, 0))
                tsc = p.stage("tensor", ft, waits=wv_t)
                if stt >= 32:
                    break
                vsub = p.stage("vector", lambda v, stt=stt: [
                    v.tensor_tensor(sc[:, (2 * hf + stt % 2) * 512:(2 * hf + stt % 2 + 1) * 512],
                                    PS[2 * hf + stt % 2][:, :],
                                    U[:, hf * 512:(hf + 1) * 512], OP.subtract)
                    for hf in range(2)], waits={"T": tsc})
                exp_marks[stt] = p.stage("scalar", lambda s, stt=stt: [
                    s.activation(pt[:, (2 * hf + stt % 2) * 512:(2 * hf + stt % 2 + 1) * 512],
                                 sc[:, (2 * hf + stt % 2) * 512:(2 * hf + stt % 2 + 1) * 512],
                                 ACT.Exp, scale=0.125) for hf in range(2)],
                    waits={"V": vsub})
            av_done = tsc

            vrz = p.stage("vector", lambda v: [
                v.reciprocal(nr[0:1, hf * 512:(hf + 1) * 512], PS2[hf][0:1, :])
                for hf in range(2)], waits={"T": av_done})
            sev = p.stage("scalar", lambda s: [
                s.activation(osb[r0:r0 + 64, hf * 512:(hf + 1) * 512],
                             PS[4 + hf][r0:r0 + 64, :], ACT.Identity)
                for hf in range(2)], waits={"T": av_done})
            trz = p.stage("tensor", lambda t: [
                t.matmul(PS2[hf][:, :], onesrowf, nr[0:1, hf * 512:(hf + 1) * 512],
                         start=True, stop=True) for hf in range(2)], waits={"V": vrz})
            vc1 = p.stage("vector", lambda v: [
                v.tensor_tensor(sc[r0:r0 + 64, hf * 512:(hf + 1) * 512],
                                osb[r0:r0 + 64, hf * 512:(hf + 1) * 512],
                                PS2[hf][r0:r0 + 64, :], OP.mult) for hf in range(2)],
                waits={"T": trz, "S": sev})
            vc2 = p.stage("vector", lambda v: v.scalar_tensor_tensor(
                sc[r0:r0 + 64, 0:512], sc[r0:r0 + 64, 512:1024], lamc[r0:r0 + 64, 0:1],
                sc[r0:r0 + 64, 0:512], OP.mult, OP.add), waits={"V": vc1})
            sv1 = p.stage("scalar", lambda s: s.activation(
                rmsb[r0:r0 + 64, :], sc[r0:r0 + 64, 0:512], ACT.Square), waits={"V": vc2})
            tv1 = p.stage("tensor", lambda t: t.matmul(
                PS2[0][:, :], ones[r0:r0 + 64, 0:128], rmsb[r0:r0 + 64, :],
                start=True, stop=True), waits={"S": sv1})
            sv2 = p.stage("scalar", lambda s: s.activation(
                rmsf[:, :], PS2[0][:, :], ACT.Sqrt, scale=1.0 / 64.0,
                bias=constf[:, 7:8]), waits={"T": tv1})
            vv2 = p.stage("vector", lambda v: v.reciprocal(rmsf[:, :], rmsf[:, :]),
                          waits={"S": sv2})
            vv3 = p.stage("vector", lambda v: v.tensor_tensor(
                sc[r0:r0 + 64, 0:512], sc[r0:r0 + 64, 0:512], rmsf[r0:r0 + 64, :],
                OP.mult), waits={"V": vv2})
            out_marks = p.stage("vector", lambda v, grp=grp: v.tensor_scalar(
                dt_sl(onb, grp // 2, NLOC, q0, q1)[r0:r0 + 64, :], sc[r0:r0 + 64, 0:512],
                nwb[r0:r0 + 64, grp:grp + 1], 0.2, OP.mult, OP.mult),
                waits={"V": vv3, "W": nw0})
        prev_v = out_marks
        prev_s = max(exp_marks.values())
    ar.pop()

    x1 = p.dma(lambda g, mark: [
        mark(g.dma_start(out=cc["in_big"].ap()[k * 128:(k + 1) * 128, :],
                         in_=dt_sl(onb, k, NLOC))) for k in range(NDT)],
        waits={"V": prev_v}, n=NDT, sem="X")

    def fcc(g):
        g.wait_ge(p.sems["X"], x1)
        if SIM_NO_CC:
            g.dma_start(out=cc["out_big"].ap()[0:DM, :],
                        in_=cc["in_big"].ap()).then_inc(p.sems["X"], 16)
            g.dma_start(out=cc["out_big"].ap()[DM:2 * DM, :],
                        in_=cc["in_big"].ap()).then_inc(p.sems["X"], 16)
            p.inc("X", 32)
            return g.engine_nop()
        return g.collective_compute(
            "AllGather", OP.bypass, replica_groups=cc["groups"],
            ins=[cc["in_big"].ap().opt()], outs=[cc["out_big"].ap().opt()])
    cc2 = p.stage("gpsimd", fcc)

    ar.push()
    ofin = ar.alloc(NDT * NLOC, BF16)
    stg = ar.alloc(SFULL, BF16)
    wsl2 = ar.alloc(NDT * 512, BF16)
    sel = ar.alloc(32, BF16)
    sm = p.dma(lambda g, mark: mark(g.dma_start(out=sel[0:64, :], in_=A["l2a_sel"])),
               waits=p.snap("V"), n=1)
    pv = p.val("V")
    for h in range(16):
        hx = p.dma(lambda g, mark, h=h: [
            mark(g.dma_start(out=stg[0:64, r * NLOC:(r + 1) * NLOC],
                             in_=cc["out_big"].ap()[r * DM + (h // 2) * 128 + (h % 2) * 64:
                                                    r * DM + (h // 2) * 128 + (h % 2) * 64 + 64, :]))
            for r in range(2)], waits={"G": cc2, "T": p.val("T"), "V": pv}, n=2, sem="X")
        tmarks = 0
        for wv in range(8):
            tt = p.stage("tensor", lambda t, wv=wv: [
                t.matmul(PS[b % 4][:, 0:32],
                         stg[0:64, (wv * 4 + b) * 128:(wv * 4 + b + 1) * 128],
                         sel[0:64, :], start=True, stop=True) for b in range(4)],
                waits={"X": hx, "W": sm, "V": tmarks})
            tmarks = p.stage("vector", lambda v, wv=wv, h=h: [
                v.tensor_copy(
                    bass.AP(ofin.tensor,
                            ofin.offset + ((wv * 4 + b) % 8) * NLOC + h * 4 + (wv * 4 + b) // 8,
                            [[ofin.ap[0][0], 128], [64, 32]]),
                    PS[b % 4][:, 0:32]) for b in range(4)],
                waits={"T": tt})
        pv = tmarks

    prev_v2 = pv
    for wv2 in range(2):
        wo2 = p.dma(lambda g, mark, wv2=wv2: mark(g.dma_start(
            out=wsl2[:, :].rearrange("p (n m) -> p n m", n=NDT),
            in_=A["l2a_wo"][:, wv2 * 512:(wv2 + 1) * 512].rearrange(
                "(n p) m -> p n m", p=128))),
            waits={"T": p.val("T"), "V": prev_v2}, n=1)
        for qc in range(4):
            q0, q1 = qc * 512, (qc + 1) * 512
            t_o = p.stage("tensor", lambda t, q0=q0, q1=q1, wv2=wv2: [
                mm_acc(t, PS[i][:, :],
                       [wsl2[:, k * 512 + i * 128:k * 512 + (i + 1) * 128]
                        for k in range(NDT)],
                       [dt_sl(ofin, k, NLOC, q0, q1) for k in range(NDT)])
                for i in range(4)], waits={"W": wo2, "V": prev_v2})
            prev_v2 = p.stage("vector", lambda v, q0=q0, q1=q1, wv2=wv2: [
                v.tensor_tensor(dt_sl(xT, wv2 * 4 + i, NLOC, q0, q1),
                                dt_sl(xT, wv2 * 4 + i, NLOC, q0, q1),
                                PS[i][:, :], OP.add) for i in range(4)],
                waits={"T": t_o})
    ar.pop()
    ar.pop()
def emit_l2_attn_b(p, st):
    A = st["A"]

    def setup(p, st, ar):
        dtab = ar.alloc(47 * 128, F32)
        bcol = ar.alloc(1, F32)
        bcolb = ar.alloc(2, BF16)
        brow = ar.alloc(1, F32)
        return dict(dtab=dtab, bcol=bcol, bcolb=bcolb, brow=brow)

    def per_group(p, st, ar, extra, grp):
        consts, constf = st["sb"]["consts"], st["sb"]["constf"]
        PS2 = st["PS2"]
        dm = p.dma(lambda g, mark: mark(g.dma_start(
            out=extra["dtab"][:, :],
            in_=A["l2b_diag"][grp])),
            waits={"V": p.val("V"), "S": p.val("S"), "G": p.val("G")}, n=1)
        vm = p.stage("vector", lambda v: [
            v.reduce_max(extra["bcol"][:, :], extra["dtab"][:, :], axis=AX,
                         apply_absolute_value=True)], waits={"W": dm})
        vm2 = p.stage("vector", lambda v: v.tensor_copy(extra["bcolb"][:, 0:1],
                                                        extra["bcol"][:, :]),
                      waits={"V": vm})
        tt = p.stage("tensor", lambda t: t.transpose(
            PS2[0][:, 0:64].bitcast(BF16)[0:1, :], extra["bcolb"][:, 0:1],
            consts[:, 128:256]), waits={"V": vm2})
        vm3 = p.stage("vector", lambda v: v.reduce_max(
            extra["brow"][0:1, :], PS2[0][:, 0:64].bitcast(BF16)[0:1, 0:128], axis=AX),
            waits={"T": tt})
        tt2 = p.stage("tensor", lambda t: t.matmul(
            PS2[1][:, 0:1], constf[0:1, 160:288], extra["brow"][0:1, :],
            start=True, stop=True), waits={"V": vm3})
        gm = p.stage("vector", lambda v: v.tensor_copy(extra["bcol"][:, :],
                                                       PS2[1][:, 0:1]),
                     waits={"T": tt2})
        return dict(dm=dm, gm=gm)

    def u_extra(p, st, ar, extra, gext, grp):
        return (extra["bcol"][:, 0:1], None)

    def bias_ops(v, extra, gext, grp, h, qc, stt, slot):
        s0 = (4 * qc - stt + 31) * 128
        return v.tensor_tensor(slot, slot, extra["dtab"][:, s0:s0 + 512], OP.add)

    emit_softmax_attn(p, st, dict(
        nh_group=1,
        wq=lambda grp: A["l2b_wq"][:, grp * 64:(grp + 1) * 64],
        wk=lambda grp: A["l2b_wk"][:, grp * 64:(grp + 1) * 64],
        wv=lambda grp: A["l2b_wv"][:, grp * 64:(grp + 1) * 64],
        biases=dict(q="l2b_bq", k="l2b_bk", v="l2b_bv"),
        setup=setup, per_group=per_group, u_extra=u_extra, bias_ops=bias_ops,
    ))
def emit_l3_attn_a(p, st):
    A = st["A"]
    JOFF = [-1, 0, 1, 15, 16, 17]

    def setup(p, st, ar):
        constf = st["sb"]["constf"]
        onesrowf = constf[0:1, 160:288]
        PS2 = st["PS2"]
        masks = ar.alloc(96 * 128, BF16)
        lamc = ar.alloc(2, F32)
        lrow = ar.alloc(2, F32)
        ar.push()
        stag = ar.alloc(2 * 128, F32)
        ta = ar.alloc(2 * 128, F32)
        tb = ar.alloc(2 * 128, F32)
        dm0 = p.dma(lambda g, mark: mark(g.dma_start(out=lrow[0:1, 0:1],
                                                     in_=A["l3a_lambda"])),
                    waits=p.snap("V", "S", "T", "G"), n=1)
        s0 = p.stage("scalar", lambda s: [
            s.activation(lrow[0:1, 1:2], lrow[0:1, 0:1], ACT.Abs, scale=9.6008),
            s.activation(lrow[0:1, 0:1], lrow[0:1, 0:1], ACT.Identity, scale=3.2)][-1],
            waits={"W": dm0})
        t0 = p.stage("tensor", lambda t: [
            t.matmul(PS2[j][:, 0:1], onesrowf, lrow[0:1, j:j + 1], start=True, stop=True)
            for j in range(2)], waits={"S": s0})
        v0 = p.stage("vector", lambda v: [
            v.tensor_copy(lamc[:, j:j + 1], PS2[j][:, 0:1]) for j in range(2)],
            waits={"T": t0})
        mark = v0
        for ch in range(48):
            i0 = ch * 2
            dmc = p.dma(lambda g, mark2, i0=i0: [
                mark2(g.dma_start(out=stag[:, j * 128:(j + 1) * 128],
                                  in_=A["l3a_dist"][(i0 + j) // 6, (i0 + j) % 6]))
                for j in range(2)], waits={"S": p.val("S"), "V": mark}, n=2)
            sa = p.stage("scalar", lambda s: [
                (s.activation(ta[:, j * 128:(j + 1) * 128], stag[:, j * 128:(j + 1) * 128],
                              ACT.Square, bias=constf[:, 5:6]),
                 s.activation(tb[:, j * 128:(j + 1) * 128], stag[:, j * 128:(j + 1) * 128],
                              ACT.Square, bias=constf[:, 6:7]))[-1] for j in range(2)],
                waits={"W": dmc})
            sb_ = p.stage("scalar", lambda s: [
                (s.activation(ta[:, j * 128:(j + 1) * 128], ta[:, j * 128:(j + 1) * 128],
                              ACT.Exp, scale=-2.0),
                 s.activation(tb[:, j * 128:(j + 1) * 128], tb[:, j * 128:(j + 1) * 128],
                              ACT.Exp, scale=-0.125))[-1] for j in range(2)],
                waits={"S": sa})
            vv = p.stage("vector", lambda v: [
                v.scalar_tensor_tensor(ta[:, j * 128:(j + 1) * 128],
                                       ta[:, j * 128:(j + 1) * 128], 2.0,
                                       tb[:, j * 128:(j + 1) * 128], OP.mult, OP.add)
                for j in range(2)], waits={"S": sb_})
            mark = p.stage("vector", lambda v, i0=i0: [
                v.tensor_scalar(masks[:, (i0 + j) * 128:(i0 + j + 1) * 128],
                                ta[:, j * 128:(j + 1) * 128], lamc[:, 0:1], None, OP.mult)
                for j in range(2)], waits={"V": vv})
        ar.pop()
        return dict(masks=masks, lamc=lamc, mark=mark)

    def u_extra(p, st, ar, extra, gext, grp):
        return (extra["lamc"][:, 1:2], None)

    def bias_ops(v, extra, gext, grp, h, qc, stt, slot):
        last = None
        for qq in range(4):
            qi = qc * 4 + qq
            for sidx, off in enumerate(JOFF):
                if stt == qi + off:
                    last = v.tensor_tensor(
                        slot[:, qq * 128:(qq + 1) * 128], slot[:, qq * 128:(qq + 1) * 128],
                        extra["masks"][:, (qi * 6 + sidx) * 128:(qi * 6 + sidx + 1) * 128],
                        OP.add)
        return last

    emit_softmax_attn(p, st, dict(
        nh_group=2,
        wq=lambda grp: A["l3a_wq"][:, grp * 128:(grp + 1) * 128],
        wk=lambda grp: A["l3a_wk"][:, grp * 128:(grp + 1) * 128],
        wv=lambda grp: A["l3a_wv"][:, grp * 128:(grp + 1) * 128],
        wo=lambda grp: A["l3a_wo"][grp * 128:(grp + 1) * 128, :],
        biases=dict(q="l3a_bq", k="l3a_bk", v="l3a_bv", o="l3a_bo"),
        setup=setup, u_extra=u_extra, bias_ops=bias_ops,
    ))
def emit_l3_ffn(p, st):
    A, sb, PS = st["A"], st["sb"], st["PS"]
    xT, xn, ar, consts = sb["xT"], sb["xn"], sb["arena"], sb["consts"]
    eye = consts[:, 128:256]
    rowsel = consts[:, 384:1408]
    ar.push()
    wgate = ar.alloc(NDT * 8, BF16)
    L = ar.alloc(16 * 8, F32)
    ex = ar.alloc(16 * 8, F32)
    l2b = ar.alloc(16 * 8, F32)
    sel = ar.alloc(16 * 8, F32)
    negm = ar.alloc(16, F32)
    negm2 = ar.alloc(16, F32)
    Zc = ar.alloc(16, F32)
    s12 = ar.alloc(16, F32)
    rD = ar.alloc(16, F32)
    cwb = ar.alloc(16 * 8, BF16)
    cwT = ar.alloc(NLOC, BF16)
    ltb = ar.alloc(NLOC, BF16)
    b2sb = ar.alloc(NDT * 128, BF16)
    w1e = ar.alloc(NDT * 2048, BF16)
    w2e = ar.alloc(16 * DM, BF16)
    b1e = ar.alloc(16, F32)
    hsb = ar.alloc(16 * 512, BF16)

    pre = p.snap()

    def fw0(g, mark):
        load_wT(g, mark, wgate, A["l3f_gate"], NDT, 8)
        mark(g.dma_start(out=b2sb[0:8, :], in_=A["l3f_b2r"]))
    w0 = p.dma(fw0, waits=pre, n=2)

    def ft1(t):
        last = None
        for c in range(4):
            last = mm_acc(t, PS[c][0:8, :],
                          [wgate[:, k * 8:(k + 1) * 8] for k in range(NDT)],
                          [dt_sl(xn, k, NLOC, c * 512, (c + 1) * 512) for k in range(NDT)])
        return last
    t1 = p.stage("tensor", ft1, waits={"W": w0, "V": p.val("V")})

    v1 = p.stage("vector", lambda v: [
        v.tensor_copy(ltb[0:8, c * 512:(c + 1) * 512], PS[c][0:8, :]) for c in range(4)
    ], waits={"T": t1})

    def ft2(t):
        last = None
        for ti in range(16):
            last = t.transpose(PS[4][:, ti * 4:(ti + 1) * 4].bitcast(BF16),
                               ltb[0:8, ti * 128:(ti + 1) * 128], eye[0:8, 0:8])
        return last
    t2 = p.stage("tensor", ft2, waits={"V": v1})

    v2 = p.stage("vector", lambda v:
                 v.tensor_copy(L[:, :], PS[4][:, 0:64].bitcast(BF16)), waits={"T": t2})

    v3 = p.stage("vector", lambda v: [
        v.reduce_max(negm[:, ti:ti + 1], L[:, ti * 8:(ti + 1) * 8], axis=AX, negate=True)
        for ti in range(16)], waits={"V": v2})

    s1 = p.stage("scalar", lambda s: [
        s.activation(ex[:, ti * 8:(ti + 1) * 8], L[:, ti * 8:(ti + 1) * 8], ACT.Exp,
                     bias=negm[:, ti:ti + 1]) for ti in range(16)], waits={"V": v3})

    v4a = p.stage("vector", lambda v: [
        v.tensor_scalar(sel[:, ti * 8:(ti + 1) * 8], L[:, ti * 8:(ti + 1) * 8],
                        negm[:, ti:ti + 1], 0.0, OP.add, OP.is_ge) for ti in range(16)],
        waits={"S": s1})
    v4b = p.stage("vector", lambda v: [
        v.scalar_tensor_tensor(l2b[:, :], sel[:, :], -1.0e30, L[:, :], OP.mult, OP.add),
    ] + [v.reduce_sum(Zc[:, ti:ti + 1], ex[:, ti * 8:(ti + 1) * 8], axis=AX)
         for ti in range(16)], waits={"V": v4a})
    v4c = p.stage("vector", lambda v: [
        v.reduce_max(negm2[:, ti:ti + 1], l2b[:, ti * 8:(ti + 1) * 8], axis=AX,
                     negate=True) for ti in range(16)], waits={"V": v4b})
    v5a = p.stage("vector", lambda v: [
        v.tensor_scalar(l2b[:, ti * 8:(ti + 1) * 8], l2b[:, ti * 8:(ti + 1) * 8],
                        negm2[:, ti:ti + 1], 0.0, OP.add, OP.is_ge) for ti in range(16)],
        waits={"V": v4c})
    v5b = p.stage("vector", lambda v:
                  v.tensor_tensor(sel[:, :], sel[:, :], l2b[:, :], OP.add),
                  waits={"V": v5a})
    v5c = p.stage("vector", lambda v:
                  v.tensor_tensor(sel[:, :], sel[:, :], ex[:, :], OP.mult),
                  waits={"V": v5b})
    v5d = p.stage("vector", lambda v: [
        v.reduce_sum(s12[:, ti:ti + 1], sel[:, ti * 8:(ti + 1) * 8], axis=AX)
        for ti in range(16)], waits={"V": v5c})
    v5e = p.stage("vector", lambda v:
                  v.scalar_tensor_tensor(rD[:, :], Zc[:, :], 1.0e-8, s12[:, :],
                                         OP.mult, OP.add), waits={"V": v5d})
    v5f = p.stage("vector", lambda v: v.reciprocal(rD[:, :], rD[:, :]),
                  waits={"V": v5e})
    v6 = p.stage("vector", lambda v: [
        v.tensor_scalar(cwb[:, ti * 8:(ti + 1) * 8], sel[:, ti * 8:(ti + 1) * 8],
                        rD[:, ti:ti + 1], None, OP.mult) for ti in range(16)],
        waits={"V": v5f})

    def ft3(t):
        last = None
        for ti in range(16):
            bk, off = 5 + ti // 8, (ti % 8) * 64
            last = t.transpose(PS[bk][:, off:off + 64].bitcast(BF16)[0:8, :],
                               cwb[:, ti * 8:(ti + 1) * 8], eye[:, 0:128])
        return last
    t3 = p.stage("tensor", ft3, waits={"V": v6})

    DEBUG_TENSORS.update(l3f_L=L, l3f_ex=ex, l3f_sel=sel, l3f_cwb=cwb, l3f_cwT=cwT,
                         l3f_ltb=ltb, l3f_negm=negm, l3f_rD=rD, l3f_hsb=hsb, l3f_s12=s12,
                         l3f_Zc=Zc)
    import os
    if os.environ.get("KDBG"):
        DEBUG_DUMP.extend([(0, cwT[0:8, :]), (1, L[:, :]), (2, ltb[0:8, :]),
                           (3, ex[:, :]), (4, sel[:, :]), (5, rD[:, :]),
                           (6, l2b[:, :]), (7, negm[:, :]), (7, negm2[:, :]),
                           (7, Zc[:, :]), (7, s12[:, :])])
    v7 = p.stage("vector", lambda v: [
        v.tensor_copy(cwT[0:8, ti * 128:(ti + 1) * 128],
                      PS[5 + ti // 8][:, (ti % 8) * 64:(ti % 8) * 64 + 64].bitcast(BF16)[0:8, :])
        for ti in range(16)], waits={"T": t3})

    prev_t = p.val("T")
    prev_v = p.val("V")
    for e in range(8):
        def fwE(g, mark):
            load_wT(g, mark, w1e, A["l3f_w1"][e], NDT, 2048)
            load_wT(g, mark, w2e, A["l3f_w2"][e], 16, DM)
            mark(g.dma_start(out=b1e[:, :],
                             in_=A["l3f_b1"][e].rearrange("n p one -> p (n one)"),
                             allow_slow_non_contiguous=True))
        wE = p.dma(fwE, waits={"T": prev_t, "V": prev_v, "W": p.val("W")}, n=3)
        for c in range(4):
            cs0, cs1 = c * 512, (c + 1) * 512
            rhs = [dt_sl(xn, k, NLOC, cs0, cs1) for k in range(NDT)]
            smarks = []
            for hp in range(16):
                bk = hp % 2
                wv = {"W": wE, "V": prev_v}
                if len(smarks) >= 2:
                    wv["S"] = smarks[-2]
                tm = p.stage("tensor", lambda t: mm_acc(
                    t, PS[bk][:, :],
                    [w1e[:, k * 2048 + hp * 128: k * 2048 + (hp + 1) * 128] for k in range(NDT)],
                    rhs), waits=wv)
                smarks.append(p.stage("scalar", lambda s: s.activation(
                    hsb[:, hp * 512:(hp + 1) * 512], PS[bk][:, :], ACT.Relu,
                    bias=b1e[:, hp:hp + 1]), waits={"T": tm}))
            tb = p.stage("tensor", lambda t: t.matmul(
                PS[2][:, :], rowsel[0:8, e * 128:(e + 1) * 128],
                cwT[0:8, cs0:cs1], start=True, stop=True),
                waits={"V": v7, "S": smarks[-1]})
            vs = p.stage("vector", lambda v: [
                v.tensor_tensor(hsb[:, hp2 * 512:(hp2 + 1) * 512],
                                hsb[:, hp2 * 512:(hp2 + 1) * 512], PS[2][:, :], OP.mult)
                for hp2 in range(16)], waits={"T": tb})
            t_o = p.stage("tensor", lambda t: [
                mm_acc(t, PS[i][:, :],
                       [w2e[:, k * DM + i * 128: k * DM + (i + 1) * 128] for k in range(16)],
                       [hsb[:, k * 512:(k + 1) * 512] for k in range(16)])
                for i in range(NDT)], waits={"V": vs})
            prev_v = p.stage("vector", lambda v: [
                v.tensor_tensor(dt_sl(xT, i, NLOC, cs0, cs1), dt_sl(xT, i, NLOC, cs0, cs1),
                                PS[i][:, :], OP.add) for i in range(NDT)],
                waits={"T": t_o})
            prev_t = t_o

    for c in range(4):
        cs0, cs1 = c * 512, (c + 1) * 512
        t4 = p.stage("tensor", lambda t: [
            t.matmul(PS[i][:, :], b2sb[0:8, i * 128:(i + 1) * 128],
                     cwT[0:8, cs0:cs1], start=True, stop=True) for i in range(NDT)],
            waits={"V": prev_v})
        prev_v = p.stage("vector", lambda v: [
            (v.tensor_tensor(dt_sl(xT, i, NLOC, cs0, cs1), dt_sl(xT, i, NLOC, cs0, cs1),
                             PS[i][:, :], OP.add),
             v.tensor_tensor(dt_sl(xT, i, NLOC, cs0, cs1), dt_sl(xT, i, NLOC, cs0, cs1),
                             dt_sl(xn, i, NLOC, cs0, cs1), OP.add))[-1]
            for i in range(NDT)], waits={"T": t4})
    ar.pop()


def exchange_small(p, st, src_ap, rows, cols, waits):
    cc = st["cc"]
    x0 = p.dma(lambda g, mark: mark(g.dma_start(
        out=cc["in_small"].ap()[0:rows, 0:cols], in_=src_ap)), waits=waits, n=1, sem="X")

    def fcc(g):
        g.wait_ge(p.sems["X"], x0)
        if SIM_NO_CC:
            g.dma_start(out=cc["out_small"].ap()[0:rows, 0:cols],
                        in_=cc["in_small"].ap()[0:rows, 0:cols]).then_inc(p.sems["X"], 16)
            g.dma_start(out=cc["out_small"].ap()[128:128 + rows, 0:cols],
                        in_=cc["in_small"].ap()[0:rows, 0:cols]).then_inc(p.sems["X"], 16)
            p.inc("X", 32)
            return g.engine_nop()
        return g.collective_compute(
            "AllGather", OP.bypass, replica_groups=cc["groups"],
            ins=[cc["in_small"].ap().opt()], outs=[cc["out_small"].ap().opt()])
    return p.stage("gpsimd", fcc)


def emit_l0_attn_a(p, st):
    A, sb, PS = st["A"], st["sb"], st["PS"]
    xT, xn, ar, consts, constf = sb["xT"], sb["xn"], sb["arena"], sb["consts"], sb["constf"]
    smask = consts[:, 256:384]
    ones = consts[:, 0:128]
    eye = consts[:, 128:256]
    mask_rem = constf[:, 0:1]
    pre = p.snap()
    ar.push()
    wsl = ar.alloc(3 * NDT * 128, BF16)
    wob = ar.alloc(NDT * 128, BF16)
    qta = ar.alloc(NDT * NLOC, BF16)
    ktp = ar.alloc(NLOC, BF16)
    knat = ar.alloc(16 * 128, BF16)
    vnat = ar.alloc(16 * 130, BF16)
    aon = ar.alloc(NDT * NLOC, BF16)
    ptile = ar.alloc(2 * 128, BF16)
    msb = ar.alloc(64, BF16)
    mstf = ar.alloc(64, F32)
    stash = ar.alloc(512, F32)
    mines = ar.alloc(512, BF16)
    bvB = ar.alloc(128, BF16)

    prev_v = p.val("V")
    prev_s = p.val("S")

    def fb(g, mark):
        for i, nm in enumerate(["wq", "wk", "wv", "wo"]):
            mark(g.dma_start(out=constf[:, 64 + i * 8:64 + (i + 1) * 8],
                             in_=A[f"l0a_{nm}b"].rearrange("n p one -> p (n one)"),
                             allow_slow_non_contiguous=True))
    wb = p.dma(fb, waits=pre, n=4)
    bofs = {nm: constf[:, 64 + i * 8:64 + (i + 1) * 8]
            for i, nm in enumerate(["wq", "wk", "wv", "wo"])}

    for grp in range(8):
        def fwg(g, mark):
            for j, nm in enumerate(["wq", "wk", "wv"]):
                mark(g.dma_start(
                    out=wsl[:, j * NDT * 128:(j + 1) * NDT * 128].rearrange(
                        "p (n m) -> p n m", n=NDT),
                    in_=A[f"l0a_{nm}w"][:, grp * 128:(grp + 1) * 128].rearrange(
                        "(n p) m -> p n m", p=128)))
        wgm = p.dma(fwg, waits={"T": p.val("T"), "V": prev_v, "S": prev_s, "W": wb}, n=3)

        bvm = p.dma(lambda g, mark: mark(g.dma_start(
            out=bvB[0:1, :], in_=A["l0a_wvb_row"][grp:grp + 1, :])),
            waits={"V": prev_v}, n=1)
        tbv = p.stage("tensor", lambda t: t.matmul(
            st["PS2"][0][:, 0:128], ones[0:1, 0:128], bvB[0:1, :],
            start=True, stop=True), waits={"W": bvm, "V": prev_v})
        vbv = p.stage("vector", lambda v: v.tensor_copy(
            bvB[:, :], st["PS2"][0][:, 0:128]), waits={"T": tbv})

        tq = p.stage("tensor", lambda t: [
            mm_acc(t, PS[c][:, :],
                   [wsl[:, k * 128:(k + 1) * 128] for k in range(NDT)],
                   [dt_sl(xn, k, NLOC, c * 512, (c + 1) * 512) for k in range(NDT)])
            for c in range(4)], waits={"W": wgm, "V": max(prev_v, vbv)})
        sq = p.stage("scalar", lambda s: [
            s.activation(dt_sl(qta, grp, NLOC, c * 512, (c + 1) * 512), PS[c][:, :],
                         ACT.Identity, bias=bofs["wq"][:, grp:grp + 1])
            for c in range(4)], waits={"T": tq})
        tk = p.stage("tensor", lambda t: [
            mm_acc(t, PS[4 + c][:, :],
                   [wsl[:, NDT * 128 + k * 128:NDT * 128 + (k + 1) * 128] for k in range(NDT)],
                   [dt_sl(xn, k, NLOC, c * 512, (c + 1) * 512) for k in range(NDT)])
            for c in range(4)], waits={"W": wgm, "V": max(prev_v, vbv)})
        skk = p.stage("scalar", lambda s: [
            s.activation(ktp[:, c * 512:(c + 1) * 512], PS[4 + c][:, :],
                         ACT.Identity, bias=bofs["wk"][:, grp:grp + 1])
            for c in range(4)], waits={"T": tk})

        vkn = max(sq, skk)
        for w in range(4):
            tkn = p.stage("tensor", lambda t, w=w: [
                t.transpose(PS[j][:, 0:64].bitcast(BF16),
                            ktp[:, (w * 4 + j) * 128:(w * 4 + j + 1) * 128],
                            eye) for j in range(4)],
                waits={"S": max(sq, skk), "V": vkn if w else 0})
            vkn = p.stage("vector", lambda v, w=w: [
                v.tensor_copy(knat[:, (w * 4 + j) * 128:(w * 4 + j + 1) * 128],
                              PS[j][:, 0:64].bitcast(BF16)) for j in range(4)],
                waits={"T": tkn})

        vm = vkn
        for w in range(2):
            tv = 0
            for k in range(NDT):
                tv = p.stage("tensor", lambda t, k=k, w=w: [
                    t.matmul(PS[b][:, 0:128],
                             dt_sl(xn, k, NLOC, (w * 8 + b) * 128, (w * 8 + b + 1) * 128),
                             wsl[:, 2 * NDT * 128 + k * 128:2 * NDT * 128 + (k + 1) * 128],
                             start=(k == 0), stop=(k == NDT - 1)) for b in range(8)],
                    waits={"V": vm} if k == 0 else {})
            vm = p.stage("vector", lambda v, w=w: [
                v.tensor_copy(vnat[:, (w * 8 + b) * 130:(w * 8 + b) * 130 + 128],
                              PS[b][:, 0:128]) for b in range(8)], waits={"T": tv})
        vm2 = p.stage("vector", lambda v: [
            v.tensor_tensor(vnat[:, s2 * 130:s2 * 130 + 128],
                            vnat[:, s2 * 130:s2 * 130 + 128], bvB[:, :], OP.add)
            for s2 in range(16)], waits={"V": vm})
        vm3 = p.stage("vector", lambda v: [
            v.memset(msb[:, :], 0.0), v.memset(mstf[:, :], 0.0)][-1], waits={"V": vm2})

        av_marks = {}
        vst = vm3
        for i in range(16):
            tsx = p.stage("tensor", lambda t, i=i: [
                t.matmul(PS[(i % 2) * 2 + h][:, 0:128],
                         ktp[h * 64:h * 64 + 64, i * 128:(i + 1) * 128],
                         qta[grp * 128:(grp + 1) * 128, i * 128:(i + 1) * 128][
                             h * 64:h * 64 + 64, :] if False else
                         dt_sl(qta, grp, NLOC, i * 128, (i + 1) * 128)[h * 64:h * 64 + 64, :],
                         start=True, stop=True) for h in range(2)],
                waits={"V": av_marks.get(i - 2, vm3)})
            vmk = p.stage("vector", lambda v, i=i: [
                v.tensor_tensor(ptile[:, h * 128:(h + 1) * 128],
                                PS[(i % 2) * 2 + h][:, 0:128], smask, OP.mult)
                for h in range(2)], waits={"T": tsx})
            if i > 0:
                vs1 = p.stage("vector", lambda v: [
                    v.tensor_tensor(mstf[h * 64:h * 64 + 64, :],
                                    mstf[h * 64:h * 64 + 64, :],
                                    PS[6 + h][h * 64:h * 64 + 64, 0:64], OP.add)
                    for h in range(2)], waits={"T": p.val("T")})
                vst = p.stage("vector", lambda v: v.tensor_copy(msb[:, :], mstf[:, :]),
                              waits={"V": vs1})
            def ftav(t, i=i):
                last = None
                for h in range(2):
                    t.matmul(PS[4][h * 64:h * 64 + 64, 0:128],
                             vnat[:, i * 130 + h * 64:i * 130 + h * 64 + 64],
                             ptile[:, h * 128:(h + 1) * 128], start=True, stop=False)
                    t.matmul(PS[4][h * 64:h * 64 + 64, 0:128],
                             msb[h * 64:h * 64 + 64, :],
                             dt_sl(qta, grp, NLOC, i * 128, (i + 1) * 128)[h * 64:h * 64 + 64, :],
                             start=False, stop=True)
                    last = t.matmul(PS[6 + h][h * 64:h * 64 + 64, 0:64],
                                    knat[:, i * 128 + h * 64:i * 128 + h * 64 + 64],
                                    vnat[:, i * 130 + h * 64:i * 130 + h * 64 + 64],
                                    start=True, stop=True)
                return last
            tav = p.stage("tensor", ftav, waits={"V": max(vmk, vst)})
            av_marks[i] = p.stage("vector", lambda v, i=i: v.tensor_copy(
                dt_sl(aon, grp, NLOC, i * 128, (i + 1) * 128), PS[4][:, 0:128]),
                waits={"T": tav})
        vs1 = p.stage("vector", lambda v: [
            v.tensor_tensor(mstf[h * 64:h * 64 + 64, :], mstf[h * 64:h * 64 + 64, :],
                            PS[6 + h][h * 64:h * 64 + 64, 0:64], OP.add)
            for h in range(2)], waits={"T": p.val("T"), "V": av_marks[15]})
        prev_v = p.stage("vector", lambda v, grp=grp: v.tensor_copy(
            stash[:, grp * 64:(grp + 1) * 64], mstf[:, :]), waits={"V": vs1})
        prev_s = max(sq, skk)

    gx = exchange_small(p, st, stash[:, 0:512], 128, 512, {"V": prev_v})
    mx = p.dma(lambda g, mark: mark(g.dma_start(
        out=mines[:, :], in_=st["cc"]["out_small"].ap()[0:128, 0:512])),
        waits={"G": gx}, n=1, sem="X")
    vmask = p.stage("vector", lambda v: v.tensor_scalar(
        mines[:, :], mines[:, :], mask_rem, None, OP.mult), waits={"X": mx})

    prev_v2 = vmask
    for grp in range(8):
        wom = p.dma(lambda g, mark, grp=grp: mark(g.dma_start(
            out=wob[:, :], in_=A["l0a_wow"][grp * 128:(grp + 1) * 128, :])),
            waits={"T": p.val("T"), "W": p.val("W")}, n=1)
        for qc in range(4):
            q0, q1 = qc * 512, (qc + 1) * 512
            tcor = p.stage("tensor", lambda t, grp=grp, q0=q0, q1=q1: [
                t.matmul(PS[4][h * 64:h * 64 + 64, :],
                         mines[h * 64:h * 64 + 64, grp * 64:(grp + 1) * 64],
                         dt_sl(qta, grp, NLOC, q0, q1)[h * 64:h * 64 + 64, :],
                         start=True, stop=True) for h in range(2)],
                waits={"V": prev_v2})
            vcor = p.stage("vector", lambda v, grp=grp, q0=q0, q1=q1: v.tensor_tensor(
                dt_sl(aon, grp, NLOC, q0, q1), dt_sl(aon, grp, NLOC, q0, q1),
                PS[4][:, :], OP.add), waits={"T": tcor})
            for wv2 in range(2):
                t_o = p.stage("tensor", lambda t, grp=grp, wv2=wv2, q0=q0, q1=q1: [
                    t.matmul(PS[i][:, :], wob[:, (wv2 * 4 + i) * 128:(wv2 * 4 + i + 1) * 128],
                             dt_sl(aon, grp, NLOC, q0, q1), start=True, stop=True)
                    for i in range(4)], waits={"V": vcor if wv2 == 0 else prev_v2, "W": wom})
                prev_v2 = p.stage("vector", lambda v, wv2=wv2, q0=q0, q1=q1: [
                    v.tensor_tensor(dt_sl(xT, wv2 * 4 + i, NLOC, q0, q1),
                                    dt_sl(xT, wv2 * 4 + i, NLOC, q0, q1),
                                    PS[i][:, :], OP.add) for i in range(4)],
                    waits={"T": t_o})
    p.stage("vector", lambda v: [
        v.tensor_scalar_add(dt_sl(xT, i, NLOC), dt_sl(xT, i, NLOC),
                            bofs["wo"][:, i:i + 1]) for i in range(NDT)],
        waits={"V": prev_v2})
    ar.pop()


def kernel(**inputs):
    ims = [host_prep(inputs, core // 2, core % 2) for core in range(8)]
    nc = build_program(ims[0])
    res = run_bass_kernel_spmd(nc, ims, core_ids=list(range(8)))
    out = np.empty((4, SFULL, DM), np.float32)
    for core in range(8):
        b, hc = core // 2, core % 2
        out[b, hc * NLOC:(hc + 1) * NLOC, :] = res.results[core]["yT"].T
    return out
